# version 34
# baseline (speedup 1.0000x reference)
import math
import numpy as np

import concourse.bass as bass
import concourse.mybir as mybir
from concourse.bass_utils import run_bass_kernel_spmd

F32 = mybir.dt.float32
BF16 = mybir.dt.bfloat16
AF = mybir.ActivationFunctionType
ALU = mybir.AluOpType
AX = mybir.AxisListType

C = 1024
H = 8
D = 128
TOWN = 2048
NSEQ = 4096
NBO = TOWN // 128
NBS = NSEQ // 128
NF = 512
EPS = 1e-5

_C = -math.log(10000.0) / 512.0
C_HI = float(np.float32(_C))
C_LO = float(np.float32(_C - C_HI))
TWO_PI = float(np.float32(2 * math.pi))
PI = float(np.float32(math.pi))
THREE_HALF_PI = float(np.float32(3 * math.pi / 2))
INV2PI = float(np.float32(1.0 / (2 * math.pi)))
CW1 = float(np.float32(np.trunc(2 * math.pi * 2**8) / 2**8))
CW2 = float(np.float32(np.trunc((2 * math.pi - CW1) * 2**21) / 2**21))
CW3 = float(np.float32(2 * math.pi - CW1 - CW2))
CW23 = float(np.float32(2 * math.pi - CW1))
HALF_PI = float(np.float32(math.pi / 2))
HP_LO = float(np.float32(math.pi / 2 - HALF_PI))

DEBUG = False
XNT_OFF = 16
COL_LH = 15
COL_RH = XNT_OFF + TOWN
XNT_W = 2080


def build_nc():
    nc = bass.Bass("TRN2", target_bir_lowering=False,
                   detect_race_conditions=False)

    xseq = nc.dram_tensor("xseq", [NSEQ, C], F32, kind="ExternalInput").ap()
    wqk = nc.dram_tensor("wqk", [9, 128, 2 * C], F32, kind="ExternalInput").ap()
    wlepe = nc.dram_tensor("wlepe", [3, 9, 128, C], F32, kind="ExternalInput").ap()
    wffn = nc.dram_tensor("wffn", [9, 128, C], F32, kind="ExternalInput").ap()
    alpha = nc.dram_tensor("alpha", [128, 1], F32, kind="ExternalInput").ap()
    poscol = nc.dram_tensor("poscol", [128, NBS], F32, kind="ExternalInput").ap()
    masklr = nc.dram_tensor("masklr", [128, 2], F32, kind="ExternalInput").ap()
    out = nc.dram_tensor("out", [TOWN, C], F32, kind="ExternalOutput").ap()
    halo_scr = nc.dram_tensor("halo_scr", [C], BF16, kind="Internal").ap()
    if DEBUG:
        dbg_x2 = nc.dram_tensor("dbg_x2", [TOWN, C], F32, kind="ExternalOutput").ap()
        dbg_qz = nc.dram_tensor("dbg_qz", [TOWN, C], F32, kind="ExternalOutput").ap()
        dbg_sm = nc.dram_tensor("dbg_sm", [NBO, 128, 80], F32, kind="ExternalOutput").ap()
        dbg_at = nc.dram_tensor("dbg_at", [TOWN, C], F32, kind="ExternalOutput").ap()

    from contextlib import ExitStack
    ctx = ExitStack()
    sb = lambda name, shp, dt: ctx.enter_context(nc.sbuf_tensor(name, shp, dt))
    psum = lambda name, shp: ctx.enter_context(nc.psum_tensor(name, shp, F32))

    wk_bf = sb("wk_bf", [128, 9, C], BF16)
    wq_bf = sb("wq_bf", [128, 9, C], BF16)
    wl_bf = sb("wl_bf", [128, 25, C], BF16)
    wf_bf = sb("wf_bf", [128, 9, C], BF16)
    xnT = sb("xnT", [128, 8, XNT_W], BF16)
    ones_blk = sb("ones_blk", [128, 128], BF16)
    theta_bc = sb("theta_bc", [128, NF], F32)
    kmb = sb("kmb", [128, C], F32)
    kv_bf = sb("kv_bf", [128, H, D], BF16)
    c256 = sb("c256", [128, NF], F32)
    s256 = sb("s256", [128, NF], F32)
    ones_f32 = sb("ones_f32", [1, 128], F32)
    ones1_bf = sb("ones1_bf", [128, 1], BF16)
    poscol_sb = sb("poscol_sb", [128, NBS], F32)
    masklr_sb = sb("masklr_sb", [128, 2], F32)
    delta_s = sb("delta_s", [1, 1], F32)
    smallf = sb("smallf", [128, 96], F32)
    onesc_f32 = smallf[:, 0:1]
    negpi = smallf[:, 1:2]
    alpha_sb = smallf[:, 2:3]
    delta_bc = smallf[:, 3:4]
    absacc = smallf[:, 4:5]
    absrow = smallf[:, 5:6]
    hpi = smallf[:, 6:7]
    eps_bc = smallf[:, 7:8]

    xblk = [sb(f"xblk{j}", [128, C], F32) for j in range(2)]
    xn_bf = [sb(f"xn_bf{j}", [128, C], BF16) for j in range(3)]
    xnT_tmp = [sb(f"xnT_tmp{j}", [128, 8, 128], BF16) for j in range(2)]
    kh = [sb(f"kh{j}", [128, C], F32) for j in range(2)]
    khb = [sb(f"khb{j}", [128, C], BF16) for j in range(2)]
    kr_bf = [sb(f"krbf{j}", [128, C], BF16) for j in range(2)]
    sinb = [sb(f"sinb{j}", [128, NF], F32) for j in range(2)]
    cosb = [sb(f"cosb{j}", [128, NF], F32) for j in range(2)]
    rm1 = sb("rm1", [128, NF], F32)
    rm2 = sb("rm2", [128, NF], F32)
    rm3 = sb("rm3", [128, NF], F32)
    rm4 = sb("rm4", [128, NF], F32)
    kint = kmb[:, 0:NF].bitcast(mybir.dt.int32)
    qzb = sb("qzb", [128, C], F32)
    angb = qzb[:, 0:NF]
    x2b = sb("x2b", [128, C], F32)
    stage = x2b[:, 0:NF]
    srow = x2b[0:1, 512:1024]
    bnst = [smallf[:, 8 + 12 * j:8 + 12 * j + 12].rearrange("p (a b) -> p a b", a=2)
            for j in range(2)]
    mv = [smallf[:, 32 + 2 * j:34 + 2 * j] for j in range(2)]
    varp = [smallf[:, 36 + j:37 + j] for j in range(2)]
    sqb = [smallf[:, 38 + j:39 + j] for j in range(2)]
    rstd = [smallf[:, 40 + j:41 + j] for j in range(2)]
    nmean = [smallf[:, 42 + j:43 + j] for j in range(2)]
    zden = [smallf[:, 48 + 8 * j:56 + 8 * j] for j in range(2)]
    zval = [smallf[:, 64 + 8 * j:72 + 8 * j] for j in range(2)]

    pk = [psum("pk0", [128, C]), psum("pk1", [128, C])]
    pkv = psum("pkv", [128, C])
    psF = psum("psF", [128, C])

    sem_names = ["qx", "qst", "qwk", "qwq", "qwl", "qc", "qti", "qhl",
                 "qt2", "qt3", "qty", "vp", "sp", "gp", "tp"]
    SEM = {n: ctx.enter_context(nc.semaphore(n)) for n in sem_names}
    CNT = {n: 0 for n in sem_names}
    OPS = {"v": [], "s": [], "t": [], "g": [], "sy": []}
    ENG_SEM = {"v": "vp", "s": "sp", "g": "gp", "t": "tp"}
    RMF = {"w": ("vp", 0)}

    def E(eng, build, waits=(), inc=None, inck=1, selfwait=False):
        wl = []
        for w in waits:
            if isinstance(w, str):
                wl.append((w, CNT[w]))
            else:
                wl.append(w)
        if selfwait and eng in ("v", "s", "g") and CNT[ENG_SEM[eng]] > 0:
            wl.append((ENG_SEM[eng], CNT[ENG_SEM[eng]]))
        if inc is not None:
            CNT[inc] += inck
        here = CNT[inc] if inc is not None else None

        def run(e, _b=build, _w=tuple(wl), _i=inc, _k=inck):
            seen = {}
            for s, v in _w:
                if v > 0 and seen.get(s, -1) < v:
                    seen[s] = v
            for s, v in seen.items():
                e.wait_ge(SEM[s], v)
            ins = _b(e)
            if _i is not None:
                ins.then_inc(SEM[_i], _k)
        OPS[{"v": "v", "s": "s", "g": "g", "t": "t", "sy": "sy"}[eng]].append(run)
        return (inc, here) if inc is not None else None

    L = {}

    def mark(name, sem):
        L[name] = (sem, CNT[sem])

    def lm(name):
        return L[name]

    E("v", lambda e: e.memset(smallf[:], 0.0), inc="vp")
    mark("smallf0", "vp")
    E("sy", lambda e: e.dma_start(out=poscol_sb[:], in_=poscol[:]), inc="qti", inck=16)
    E("sy", lambda e: e.dma_start(out=masklr_sb[:], in_=masklr[:]), inc="qti", inck=16)
    E("sy", lambda e: e.dma_start(out=alpha_sb, in_=alpha[:]), waits=[lm("smallf0")], inc="qti", inck=16)
    mark("tiny_inputs", "qti")

    E("v", lambda e: e.memset(ones_f32[:], 1.0), inc="vp")
    E("v", lambda e: e.memset(onesc_f32, 1.0), inc="vp")
    E("v", lambda e: e.memset(ones1_bf[:], 1.0), inc="vp")
    E("v", lambda e: e.memset(negpi, -PI), inc="vp")
    E("v", lambda e: e.memset(hpi, HALF_PI), inc="vp")
    E("v", lambda e: e.tensor_scalar_add(hpi, hpi, HP_LO),
      waits=[("vp", CNT["vp"])], inc="vp")
    E("v", lambda e: e.memset(eps_bc, EPS), inc="vp")
    E("v", lambda e: e.memset(ones_blk[:], 1.0), inc="vp")
    E("v", lambda e: e.memset(absacc, 0.0), inc="vp")
    mark("consts", "vp")

    kidx = srow[:, 0:NF]
    tmp_r = theta_bc[0:1, 0:NF]
    E("g", lambda e: e.iota(kidx, [[1, NF]], channel_multiplier=0,
                            allow_small_or_imprecise_dtypes=True), inc="gp")
    mark("kidx", "gp")
    E("s", lambda e: e.activation(tmp_r, kidx, AF.Exp, scale=C_HI),
      waits=[lm("kidx")], inc="sp")
    mark("theta_raw", "sp")
    E("v", lambda e: e.tensor_scalar(kidx, kidx, C_LO, 1.0, ALU.mult, ALU.add),
      waits=[lm("kidx"), lm("theta_raw")], inc="vp")
    E("v", lambda e: e.tensor_tensor(tmp_r, tmp_r, kidx, ALU.mult),
      waits=[lm("theta_raw")], inc="vp")
    mark("theta_row", "vp")
    E("t", lambda e: e.matmul(pk[0][:, 0:512], ones_f32[:], tmp_r, start=True, stop=True),
      waits=[lm("theta_row")], inc="tp")
    mark("theta_ps", "tp")
    E("v", lambda e: e.tensor_copy(theta_bc[:], pk[0][:, 0:512].rearrange("p n -> p n")),
      waits=[lm("theta_ps")], inc="vp")
    mark("theta_bc", "vp")
    mark("pk0_free0", "vp")

    for kc in range(9):
        E("g", lambda e, kc=kc: e.dma_start(out=wk_bf[:, kc, :], in_=wqk[kc, :, C:2 * C]),
          inc="qwk", inck=16, selfwait=False)
    mark("wk_loaded", "qwk")

    wjobs = []
    for kc in range(9):
        wjobs.append(("wq", kc))
    for kc in range(8):
        wjobs.append(("fstat", kc))
    def lidx(t, kc):
        return {0: kc, 1: 8 + kc, 2: 17 + kc}[t]
    for t in range(3):
        for kc in range(9 if t == 1 else 8):
            wjobs.append(("lepe", t, kc))

    stage_h = [stage, x2b[:, 512:1024]]
    stage_users = {0: 0, 1: 0}

    def emit_wjob(job, jidx):
        kind = job[0]
        if kind == "wq":
            kc = job[1]
            E("g", lambda e, kc=kc: e.dma_start(out=wq_bf[:, kc, :], in_=wqk[kc, :, 0:C]),
              inc="qwq", inck=16, selfwait=False)
            if kc == 8:
                mark("wq_loaded", "qwq")
        elif kind == "lepe":
            t, kc = job[1], job[2]
            E("g", lambda e, t=t, kc=kc: e.dma_start(out=wl_bf[:, lidx(t, kc), :],
                                                     in_=wlepe[t, kc, :, :]),
              inc="qwl", inck=16, selfwait=False)
            if t == 2 and kc == 7:
                mark("lepe_loaded", "qwl")
        elif kind == "fstat":
            kc = job[1]
            for hf in range(2):
                stg = stage_h[hf]
                E("s", lambda e, kc=kc, hf=hf, stg=stg: e.dma_start(
                    out=stg[:], in_=wffn[kc, :, 512 * hf:512 * (hf + 1)]),
                  waits=[("vp", stage_users[hf])], inc="qc", inck=16, selfwait=False)
                E("v", lambda e, stg=stg: e.tensor_reduce(absrow, stg[:], AX.X, ALU.add,
                                                          apply_absolute_value=True),
                  waits=["qc"], inc="vp")
                E("v", lambda e: e.tensor_add(absacc, absacc, absrow),
                  waits=[("vp", CNT["vp"])], inc="vp")
                mark("fstat_done", "vp")
                stage_users[hf] = CNT["vp"]
    def emit_ln_stats(p, src_ap, waits):
        E("v", lambda e: e.bn_stats(bnst[p][:, 0, :], src_ap[:, 0:512]), waits=waits, inc="vp")
        E("v", lambda e: e.bn_stats(bnst[p][:, 1, :], src_ap[:, 512:1024]), waits=waits, inc="vp")
        E("v", lambda e: e.bn_aggr(mv[p], bnst[p][:]), waits=[("vp", CNT["vp"])], inc="vp")
        mark("lnaggr", "vp")
        E("s", lambda e: e.activation(sqb[p], mv[p][:, 1:2], AF.Ln, bias=eps_bc),
          waits=[lm("lnaggr"), lm("consts"), ("sp", CNT["sp"])], inc="sp")
        E("s", lambda e: e.activation(rstd[p], sqb[p], AF.Exp, scale=-0.5),
          waits=[("sp", CNT["sp"])], inc="sp")
        mark("lnS", "sp")
        E("v", lambda e: e.scalar_tensor_tensor(nmean[p], mv[p][:, 0:1], -1.0,
                                                rstd[p], ALU.mult, ALU.mult),
          waits=[lm("lnS"), ("vp", CNT["vp"])], inc="vp")
        mark("ln_chain", "vp")

    def emit_full_tables(dst_s, dst_c, mark_name, possrc, wextra=()):
        if possrc[0] == "ap":
            ib = possrc[1]
            E("v", lambda e, i=ib: e.tensor_scalar_mul(angb[:], theta_bc[:],
                                                       poscol_sb[:, i:i + 1]),
              waits=[lm("theta_bc"), lm("tiny_inputs"), ("gp", CNT["gp"]),
                     ("sp", CNT["sp"]), ("vp", CNT["vp"])] + list(wextra),
              inc="vp")
        else:
            f = possrc[1]
            E("v", lambda e, f=f: e.tensor_single_scalar(angb[:], theta_bc[:], f,
                                                         ALU.mult),
              waits=[lm("theta_bc"), ("gp", CNT["gp"]), ("sp", CNT["sp"]),
                     ("vp", CNT["vp"])] + list(wextra),
              inc="vp")
        E("v", lambda e: e.tensor_single_scalar(dst_c[:], angb[:], INV2PI, ALU.mult),
          waits=[("vp", CNT["vp"])], inc="vp")
        E("v", lambda e: e.tensor_copy(kint[:], dst_c[:]),
          waits=[("vp", CNT["vp"])], inc="vp")
        E("v", lambda e: e.tensor_copy(rm1[:], kint[:]),
          waits=[("vp", CNT["vp"])], inc="vp")
        E("v", lambda e: e.scalar_tensor_tensor(dst_c[:], rm1[:], -CW1, angb[:],
                                                ALU.mult, ALU.add),
          waits=[("vp", CNT["vp"])], inc="vp")
        E("v", lambda e: e.scalar_tensor_tensor(dst_c[:], rm1[:], -CW2, dst_c[:],
                                                ALU.mult, ALU.add),
          waits=[("vp", CNT["vp"])], inc="vp")
        E("v", lambda e: e.scalar_tensor_tensor(dst_s[:], rm1[:], -CW3, dst_c[:],
                                                ALU.mult, ALU.add),
          waits=[("vp", CNT["vp"])], inc="vp")
        E("v", lambda e: e.tensor_single_scalar(rm2[:], dst_s[:], PI, ALU.is_gt),
          waits=[("vp", CNT["vp"])], inc="vp")
        E("v", lambda e: e.scalar_tensor_tensor(dst_s[:], rm2[:], -CW1, dst_s[:],
                                                ALU.mult, ALU.add),
          waits=[("vp", CNT["vp"])], inc="vp")
        E("v", lambda e: e.scalar_tensor_tensor(dst_s[:], rm2[:], -CW23, dst_s[:],
                                                ALU.mult, ALU.add),
          waits=[("vp", CNT["vp"])], inc="vp")
        E("v", lambda e: e.tensor_scalar(dst_s[:], dst_s[:], -PI, PI, ALU.max, ALU.min),
          waits=[("vp", CNT["vp"])], inc="vp")
        mark("tbl_v", "vp")
        E("s", lambda e: e.activation(dst_c[:], dst_s[:], AF.Abs),
          waits=[lm("tbl_v")], inc="sp")
        E("s", lambda e: e.activation(dst_s[:], dst_s[:], AF.Sin), inc="sp")
        E("s", lambda e: e.activation(dst_c[:], dst_c[:], AF.Sin, bias=hpi, scale=-1.0),
          waits=[lm("consts")], inc="sp")
        mark(mark_name, "sp")
        RMF["w"] = ("vp", CNT["vp"])

    def emit_delta_tables(p, iblk, sp_edge=False, on_g=False):
        sbp, cbp = sinb[p], cosb[p]
        w = [lm(f"tblr_{iblk - 2}"), lm("d256"), RMF["w"]]
        if sp_edge:
            w.append(("sp", CNT["sp"]))
        if on_g:
            wg = w + [("vp", CNT["vp"])]
            E("g", lambda e: e.tensor_tensor(rm1[:], sbp[:], c256[:], ALU.mult),
              waits=wg, inc="gp")
            E("g", lambda e: e.tensor_tensor(rm2[:], cbp[:], s256[:], ALU.mult),
              inc="gp")
            E("g", lambda e: e.tensor_tensor(rm3[:], cbp[:], c256[:], ALU.mult),
              inc="gp")
            E("g", lambda e: e.tensor_tensor(rm4[:], sbp[:], s256[:], ALU.mult),
              inc="gp")
            E("g", lambda e: e.tensor_add(sbp[:], rm1[:], rm2[:]), inc="gp")
            E("g", lambda e: e.tensor_sub(cbp[:], rm3[:], rm4[:]), inc="gp")
            mark(f"tblr_{iblk}", "gp")
            RMF["w"] = ("gp", CNT["gp"])
            return
        E("g", lambda e: e.tensor_tensor(rm1[:], sbp[:], c256[:], ALU.mult),
          waits=w, inc="gp")
        E("g", lambda e: e.tensor_tensor(rm2[:], cbp[:], s256[:], ALU.mult),
          inc="gp")
        mark(f"dlt_{iblk}", "gp")
        E("v", lambda e: e.tensor_tensor(rm3[:], cbp[:], c256[:], ALU.mult),
          waits=w + [("gp", CNT["gp"])], inc="vp")
        E("v", lambda e: e.tensor_tensor(rm4[:], sbp[:], s256[:], ALU.mult),
          inc="vp")
        E("v", lambda e: e.tensor_add(sbp[:], rm1[:], rm2[:]),
          waits=[lm(f"dlt_{iblk}")], inc="vp")
        E("v", lambda e: e.tensor_sub(cbp[:], rm3[:], rm4[:]),
          inc="vp")
        mark(f"tblr_{iblk}", "vp")
        RMF["w"] = ("vp", CNT["vp"])

    def emit_rope(p, src, dst_bf, extra_waits, tag, tblk, sp_edge=False):
        sbp, cbp = sinb[p], cosb[p]
        w = [lm(f"tblr_{tblk}"), RMF["w"]] + list(extra_waits)
        if sp_edge:
            w.append(("sp", CNT["sp"]))
        E("g", lambda e: e.tensor_tensor(rm1[:], cbp[:], src[:, 0:512], ALU.mult),
          waits=w, inc="gp")
        E("g", lambda e: e.tensor_tensor(rm2[:], sbp[:], src[:, 512:1024], ALU.mult),
          waits=w, inc="gp")
        mark(f"rm12_{tag}", "gp")
        E("v", lambda e: e.tensor_tensor(rm3[:], cbp[:], src[:, 512:1024], ALU.mult),
          waits=w, inc="vp")
        E("v", lambda e: e.tensor_tensor(rm4[:], sbp[:], src[:, 0:512], ALU.mult),
          inc="vp")
        mark(f"rm34_{tag}", "vp")
        E("v", lambda e: e.tensor_sub(dst_bf[:, 0:512], rm1[:], rm2[:]),
          waits=[lm(f"rm12_{tag}")], inc="vp")
        E("v", lambda e: e.tensor_add(dst_bf[:, 512:1024], rm3[:], rm4[:]),
          inc="vp")
        RMF["w"] = ("vp", CNT["vp"])

    def emit_ksumkv(i):
        p = i & 1
        for n in range(2):
            E("t", lambda e, n=n, i=i, p=p: e.matmul(
                psF[0:1, 512 * n:512 * (n + 1)], ones1_bf[:],
                khb[p][:, 512 * n:512 * (n + 1)],
                start=(i == 0), stop=(i == NBS - 1), skip_group_check=True),
              waits=[lm(f"khb_{i}"), lm("consts")], inc="tp" if n == 1 else None)
        mark(f"khb_used_{i}", "tp")
        for h in range(H):
            E("t", lambda e, h=h, i=i, p=p, p3=i % 3: e.matmul(
                pkv[:, D * h:D * (h + 1)], kr_bf[p][:, D * h:D * (h + 1)],
                xn_bf[p3][:, D * h:D * (h + 1)],
                start=(i == 0 and h % 4 == 0), stop=(i == NBS - 1 and h == H - 1),
                skip_group_check=True),
              waits=[lm(f"kr_{i}"), lm(f"xn_{i}")] if h == 0 else (),
              inc="tp" if h == H - 1 else None)
        mark(f"kv_{i}", "tp")
        mark(f"kr_used_{i}", "tp")
        mark(f"xn_used_{i}", "tp")

    def emit_xload(i):
        w = [lm(f"xblk_free_{i - 2}")] if i >= 2 else []
        E("sy", lambda e, i=i, p=i & 1: e.dma_start(out=xblk[p][:], in_=xseq[128 * i:128 * (i + 1), :]),
          waits=w, inc="qx", inck=16, selfwait=False)
        mark(f"xload_{i}", "qx")

    def emit_kFx(i):
        p = i & 1
        p3 = i % 3
        emit_ln_stats(p, xblk[p], [lm(f"xload_{i}")])
        wxn = [lm("ln_chain"), lm(f"xload_{i}")]
        if i >= 3:
            wxn.append(lm(f"xn_used_{i - 3}"))
        if i == NBO + 3:
            wxn.append(lm("halo_r_dma"))
        E("s", lambda e: e.activation(xn_bf[p3][:], xblk[p][:], AF.Identity,
                                      scale=rstd[p], bias=nmean[p]),
          waits=wxn, inc="sp")
        mark(f"xn_{i}", "sp")
        mark(f"xblk_free_{i}", "sp")
        if i < NBO:
            dst = xnT[:, :, XNT_OFF + 128 * i: XNT_OFF + 128 * (i + 1)]
        else:
            dst = xnT_tmp[p][:, :, :]
        wtr = [lm(f"xn_{i}")]
        if i >= NBO + 2:
            wtr.append(lm(f"kstat_used_{i - 2}"))
        E("sy", lambda e, dst=dst, p3=p3: e.dma_start_transpose(dst, xn_bf[p3][:]),
          waits=wtr, inc="qt2", inck=16)
        mark(f"xnT_{i}", "qt2")
        if i + 2 < NBS:
            emit_xload(i + 2)

    def emit_kFp(i):
        p = i & 1
        wmm = [lm(f"xnT_{i}"), lm("wk_loaded"), lm("consts")]
        if i == FD + 1:
            wmm.append(lm("pk1_free_delta"))
        if i == 0:
            wmm.append(lm("pk0_free0"))
        if i >= 2:
            wmm.append(lm(f"pk_free_{i - 2}"))
        first = True
        for kc in range(9):
            if kc == 8:
                sap = ones_blk[:]
            elif i < NBO:
                sap = xnT[:, kc, XNT_OFF + 128 * i: XNT_OFF + 128 * (i + 1)]
            else:
                sap = xnT_tmp[p][:, kc, :]
            for n in range(2):
                isl = (kc == 8 and n == 1)
                E("t", lambda e, sap=sap, kc=kc, n=n, p=p: e.matmul(
                    pk[p][:, 512 * n:512 * (n + 1)], sap,
                    wk_bf[:, kc, 512 * n:512 * (n + 1)],
                    start=(kc == 0), stop=(kc == 8), skip_group_check=True),
                  waits=wmm if first else (), inc="tp" if isl else None)
                first = False
        mark(f"kproj_{i}", "tp")
        mark(f"kstat_used_{i}", "tp")
    def emit_kR1(i):
        p = i & 1
        wxm = [lm(f"kproj_{i}")]
        if i >= 2:
            wxm += [lm(f"khb_{i - 2}")]
        E("s", lambda e: e.activation(kh[p][:], pk[p][:], AF.Exp),
          waits=wxm, inc="sp")
        mark(f"kexp_{i}", "sp")
        E("v", lambda e: e.tensor_scalar_min(kh[p][:], kh[p][:], 1.0),
          waits=[lm(f"kexp_{i}")], inc="vp")
        wst = [lm(f"kproj_{i}")]
        if i >= 2:
            wst.append(lm(f"khb_used_{i - 2}"))
        E("v", lambda e: e.scalar_tensor_tensor(khb[p][:], pk[p][:], 0.0, kh[p][:],
                                                ALU.max, ALU.add),
          waits=wst, inc="vp")
        mark(f"kh_{i}", "vp")
        mark(f"khb_{i}", "vp")
        mark(f"pk_free_{i}", "vp")
    def emit_kR2(i):
        p = i & 1
        wrk = [lm(f"khb_{i}")]
        if i >= 2:
            wrk.append(lm(f"kr_used_{i - 2}"))
        emit_rope(p, khb[p], kr_bf[p], wrk, f"k{i}", i)
        mark(f"kr_{i}", "vp")
        if i + 2 < NBS:
            if i + 2 in (NBO, NBO + 1):
                emit_full_tables(sinb[p], cosb[p], f"tblr_{i + 2}", ("ap", i + 2))
            else:
                emit_delta_tables(p, i + 2)
        if i == NBO:
            E("sy", lambda e, p3=i % 3: e.dma_start(out=halo_scr[:], in_=xn_bf[p3][0:1, :]),
              waits=[lm(f"xn_{i}")], inc="qhl", inck=16)
            E("sy", lambda e: e.dma_start(
                out=xnT[:, :, COL_RH:COL_RH + 1],
                in_=halo_scr.rearrange("(b pp) -> pp b", pp=128)[:, :, None],
                allow_slow_non_contiguous=True),
              waits=["qhl"], inc="qhl", inck=16)
            mark("halo_r_dma", "qhl")
            E("v", lambda e: e.tensor_scalar_mul(
                xnT[:, :, COL_RH:COL_RH + 1], xnT[:, :, COL_RH:COL_RH + 1],
                masklr_sb[:, 1:2]),
              waits=[lm("halo_r_dma"), lm("tiny_inputs")], inc="vp")
            mark("halo_r", "vp")
        if i == NBS - 1:
            E("sy", lambda e, p3=i % 3: e.dma_start(out=halo_scr[:], in_=xn_bf[p3][127:128, :]),
              waits=[lm(f"xn_{i}")], inc="qhl", inck=16)
            E("sy", lambda e: e.dma_start(
                out=xnT[:, :, COL_LH:COL_LH + 1],
                in_=halo_scr.rearrange("(b pp) -> pp b", pp=128)[:, :, None],
                allow_slow_non_contiguous=True),
              waits=["qhl"], inc="qhl", inck=16)
            mark("halo_l_dma", "qhl")
            E("v", lambda e: e.tensor_scalar_mul(
                xnT[:, :, COL_LH:COL_LH + 1], xnT[:, :, COL_LH:COL_LH + 1],
                masklr_sb[:, 0:1]),
              waits=[lm("halo_l_dma"), lm("tiny_inputs")], inc="vp")
            mark("halo_l", "vp")
    sgn_c = x2b[:, 512:1024]
    FD = 10

    def emit_fdelta():
        E("t", lambda e: e.matmul(pk[1][0:1, 0:1], absacc, onesc_f32,
                                  start=True, stop=True, skip_group_check=True),
          waits=[lm("fstat_done"), lm("consts"), lm(f"pk_free_{FD - 1}")], inc="tp")
        mark("fdelta_ps", "tp")
        E("v", lambda e: e.tensor_scalar_mul(delta_s[:], pk[1][0:1, 0:1], 0.7 / float(C * C)),
          waits=[lm("fdelta_ps")], inc="vp")
        mark("fdelta_s", "vp")
        E("t", lambda e: e.matmul(pk[1][:, 0:1], ones_f32[:], delta_s[:],
                                  start=True, stop=True, skip_group_check=True),
          waits=[lm("fdelta_s")], inc="tp")
        mark("fdelta_bc", "tp")
        E("v", lambda e: e.tensor_copy(delta_bc, pk[1][:, 0:1]),
          waits=[lm("fdelta_bc")], inc="vp")
        mark("delta_bc", "vp")
        mark("pk1_free_delta", "vp")

    fjobs = list(range(18))
    fb_last = {"v": 0, "s": 0}

    def emit_fjob(j):
        kc, hf = j // 2, j % 2
        sl = np.s_[512 * hf:512 * (hf + 1)]
        wd = [("vp", fb_last["v"]), ("sp", fb_last["s"])]
        if j == 0:
            wd.append(("vp", stage_users[0]))
        E("s", lambda e, kc=kc, hf=hf: e.dma_start(
            out=stage[:], in_=wffn[kc, :, 512 * hf:512 * (hf + 1)]),
          waits=wd, inc="qc", inck=16, selfwait=False)
        mark("fb_dma", "qc")
        if kc < 8:
            wsg = [lm("fb_dma")]
            if j == 0:
                wsg.append(("vp", stage_users[1]))
            E("s", lambda e: e.sign(sgn_c[:], stage[:]), waits=wsg, inc="sp")
            E("s", lambda e: e.activation(stage[:], stage[:], AF.Abs), inc="sp")
            mark("fb_abs", "sp")
            E("v", lambda e: e.tensor_single_scalar(stage[:], stage[:], delta_bc,
                                                    ALU.is_gt),
              waits=[lm("fb_abs"), lm("delta_bc"), ("vp", CNT["vp"])], inc="vp")
            E("v", lambda e, kc=kc, sl=sl: e.scalar_tensor_tensor(
                wf_bf[:, kc, sl], sgn_c[:], alpha_sb, stage[:],
                ALU.mult, ALU.mult),
              waits=[lm("tiny_inputs")], inc="vp")
        else:
            E("v", lambda e, kc=kc, sl=sl: e.tensor_copy(wf_bf[:, kc, sl], stage[:]),
              waits=[lm("fb_dma"), ("vp", CNT["vp"])], inc="vp")
            if hf == 1:
                mark("ffn_built", "vp")
        fb_last["v"] = CNT["vp"]
        fb_last["s"] = CNT["sp"]

    emit_full_tables(s256, c256, "d256", ("imm", 256.0))
    emit_full_tables(sinb[0], cosb[0], "tblr_0", ("ap", 0))
    emit_full_tables(sinb[1], cosb[1], "tblr_1", ("ap", 1))
    emit_xload(0)
    emit_xload(1)
    emit_kFx(0)
    emit_kFx(1)
    emit_kFp(0)
    for i in range(NBS):
        if i >= 1:
            emit_ksumkv(i - 1)
        if i == FD:
            emit_fdelta()
        emit_kR1(i)
        if i + 2 < NBS:
            emit_kFx(i + 2)
        if i + 1 < NBS:
            emit_kFp(i + 1)
        emit_kR2(i)
        for _ in range(2):
            if wjobs:
                emit_wjob(wjobs.pop(0), 0)
        if i >= FD + 1 and fjobs:
            emit_fjob(fjobs.pop(0))
    emit_ksumkv(NBS - 1)

    while wjobs:
        emit_wjob(wjobs.pop(0), 0)

    E("s", lambda e: e.mul(kv_bf[:].rearrange("p h d -> p (h d)"), pkv[:], 1.0 / 64.0),
      waits=[lm(f"kv_{NBS - 1}")], inc="sp")
    mark("kv_bf", "sp")
    emit_full_tables(sinb[0], cosb[0], "tblr_0", ("ap", 0))
    emit_full_tables(sinb[1], cosb[1], "tblr_1", ("ap", 1))
    for n in range(2):
        E("v", lambda e, n=n: e.tensor_scalar_mul(srow[:], psF[0:1, 512 * n:512 * (n + 1)],
                                                  1.0 / float(NSEQ)),
          waits=[lm(f"khb_used_{NBS - 1}"), ("tp", CNT["tp"]), ("vp", CNT["vp"])], inc="vp")
        mark("kmean_row", "vp")
        E("t", lambda e, n=n: e.matmul(pk[0][:, 512 * n:512 * (n + 1)], ones_f32[:],
                                       srow[:], start=True, stop=True),
          waits=[lm("kmean_row"), lm(f"pk_free_{NBS - 2}")], inc="tp")
    mark("psF_free_boot", "vp")
    mark("kmb_ps", "tp")
    E("v", lambda e: e.tensor_copy(kmb[:], pk[0][:]), waits=[lm("kmb_ps")], inc="vp")
    mark("kmb", "vp")
    mark("pk_free_boot0", "vp")

    def emit_xres(i):
        w = [("qst", CNT["qst"])]
        if i < 2:
            w.append(lm(f"xblk_free_{NBS - 2 + i}"))
        E("sy", lambda e, i=i, p=i & 1: e.dma_start(out=xblk[p][:], in_=xseq[128 * i:128 * (i + 1), :]),
          waits=w, inc="qx", inck=16, selfwait=False)
        mark(f"xres_{i}", "qx")

    def emit_qA(i):
        p = i & 1
        wmm = [lm("wq_loaded")]
        if i == 0:
            wmm.append(lm("pk_free_boot0"))
        else:
            wmm.append(lm(f"qpk_free_{i - 1}"))
        first = True
        for kc in range(9):
            sap = (lambda kc=kc, i=i: ones_blk[:] if kc == 8 else
                   xnT[:, kc, XNT_OFF + 128 * i: XNT_OFF + 128 * (i + 1)])()
            for n in range(2):
                isl = (kc == 8 and n == 1)
                E("t", lambda e, sap=sap, kc=kc, n=n: e.matmul(
                    pk[0][:, 512 * n:512 * (n + 1)], sap,
                    wq_bf[:, kc, 512 * n:512 * (n + 1)],
                    start=(kc == 0), stop=(kc == 8)),
                  waits=wmm if first else (), inc="tp" if isl else None)
                first = False
        mark(f"qproj_{i}", "tp")
        wxm = [lm(f"qproj_{i}")]
        if i >= 2:
            wxm += [lm(f"qh_{i - 2}"), lm(f"rm34_q{i - 2}")]
        else:
            wxm += [lm(f"khb_{NBS - 2 + (i & 1)}")]
        E("s", lambda e: e.activation(kh[p][:], pk[0][:], AF.Exp),
          waits=wxm, inc="sp")
        mark(f"qexp_{i}", "sp")
        E("v", lambda e: e.tensor_scalar_min(kh[p][:], kh[p][:], 1.0),
          waits=[lm(f"qexp_{i}")], inc="vp")
        E("v", lambda e: e.scalar_tensor_tensor(kh[p][:], pk[0][:], 0.0, kh[p][:],
                                                ALU.max, ALU.add),
          waits=[lm(f"qproj_{i}")], inc="vp")
        mark(f"qh_{i}", "vp")
        mark(f"qpk_free_{i}", "vp")
        E("g", lambda e: e.tensor_tensor(qzb[:], kh[p][:], kmb[:], ALU.mult),
          waits=[lm("kmb"), lm(f"qh_{i}"), ("vp", CNT["vp"]),
                 ("qst", CNT["qst"])], inc="gp")
        mark(f"prod_{i}", "gp")
        E("v", lambda e: e.tensor_reduce(
            zden[p], qzb.rearrange("pp (s h j) -> pp h s j", s=2, h=H, j=64),
            AX.XY, ALU.add), waits=[lm(f"prod_{i}"), ("vp", CNT["vp"])], inc="vp")
        E("v", lambda e: e.tensor_scalar_add(zden[p], zden[p], 1e-6),
          waits=[("vp", CNT["vp"])], inc="vp")
        E("v", lambda e: e.reciprocal(zval[p], zden[p]),
          waits=[("vp", CNT["vp"])], inc="vp")
        mark(f"z_{i}", "vp")
        if DEBUG:
            E("s", lambda e, i=i: e.dma_start(out=dbg_sm[i], in_=smallf[:, 0:80]),
              waits=[lm(f"z_{i}")], inc="qst", inck=16, selfwait=False)
        wr = [lm(f"qh_{i}")]
        if i == 0:
            wr.append(lm(f"khb_used_{NBS - 1}"))
        emit_rope(p, kh[p], qzb, wr, f"q{i}", i, sp_edge=(i < 2))
        mark(f"qr_{i}", "vp")
        wqz = [lm(f"qr_{i}"), lm(f"z_{i}")]
        if i >= 2:
            wqz.append(lm(f"qrz_used_{i - 2}"))
        else:
            wqz.append(lm(f"khb_used_{NBS - 2 + (i & 1)}"))
        E("v", lambda e, p=p: e.tensor_tensor(
            khb[p].rearrange("pp (h j) -> pp h j", h=H),
            qzb.rearrange("pp (h j) -> pp h j", h=H),
            zval[p][:, :, None].broadcast_to((128, H, D)), ALU.mult),
          waits=wqz, inc="vp")
        mark(f"qrz_{i}", "vp")
        if DEBUG:
            E("s", lambda e, i=i: e.dma_start(out=dbg_qz[128 * i:128 * (i + 1), :],
                                              in_=qzb[:]),
              waits=[lm(f"qr_{i}")], inc="qst", inck=16, selfwait=False)
        wtr = [lm(f"qrz_{i}")]
        if i >= 2:
            wtr.append(lm(f"qrT_used_{i - 2}"))
        else:
            wtr.append(lm(f"kstat_used_{NBS - 2 + (i & 1)}"))
        E("sy", lambda e, p=p: e.dma_start_transpose(xnT_tmp[p][:, :, :], khb[p][:]),
          waits=wtr, inc="qt3", inck=16)
        mark(f"qrT_{i}", "qt3")
        mark(f"qrz_used_{i}", "qt3")
        if i + 2 < NBO:
            emit_delta_tables(p, i + 2, sp_edge=(i < 2), on_g=True)
    def emit_qL(i):
        p = i & 1
        LL = [pkv, psF][p]
        wl = [lm("lepe_loaded"), lm("halo_l"), lm("halo_r")]
        if i == 0:
            wl.append(lm("kv_bf"))
        elif i == 1:
            wl.append(lm("psF_free_boot"))
        if i >= 2:
            wl.append(lm(f"ata2_{i - 2}"))
        first = True
        for t in range(3):
            for kc in range(9):
                if kc == 8 and t != 1:
                    continue
                sap = (lambda t=t, kc=kc, i=i: ones_blk[:] if kc == 8 else
                       xnT[:, kc, XNT_OFF + 128 * i + t - 1: XNT_OFF + 128 * i + t - 1 + 128])()
                for n in range(2):
                    isl = (t == 2 and kc == 7 and n == 1)
                    E("t", lambda e, sap=sap, li=lidx(t, kc), t=t, kc=kc, n=n, LL=LL: e.matmul(
                        LL[:, 512 * n:512 * (n + 1)], sap,
                        wl_bf[:, li, 512 * n:512 * (n + 1)],
                        start=(t == 0 and kc == 0), stop=False,
                        skip_group_check=True),
                      waits=wl if first else (), inc="tp" if isl else None)
                    first = False
        mark(f"lepe_{i}", "tp")
    def emit_qAT(i):
        p = i & 1
        LL = [pkv, psF][p]
        for h in range(H):
            E("t", lambda e, h=h, p=p, LL=LL: e.matmul(
                LL[:, D * h:D * (h + 1)], xnT_tmp[p][:, h, :], kv_bf[:, h, :],
                start=False, stop=(h == H - 1), skip_group_check=True),
              waits=[lm(f"qrT_{i}"), lm("kv_bf"), lm(f"lepe_{i}")] if h == 0 else (),
              inc="tp" if h == H - 1 else None)
        mark(f"attn_{i}", "tp")
        mark(f"qrT_used_{i}", "tp")
    def emit_qB1(i):
        p = i & 1
        LL = [pkv, psF][p]
        if DEBUG:
            E("v", lambda e, LL=LL: e.tensor_copy(x2b[:], LL[:]),
              waits=[lm(f"attn_{i}"), ("qst", CNT["qst"]), ("vp", CNT["vp"])], inc="vp")
            mark(f"dbgat_{i}", "vp")
            E("s", lambda e, i=i: e.dma_start(out=dbg_at[128 * i:128 * (i + 1), :],
                                              in_=x2b[:]),
              waits=[lm(f"dbgat_{i}")], inc="qst", inck=16, selfwait=False)
        emit_ln_stats(p, LL, [lm(f"attn_{i}")])
        E("s", lambda e, LL=LL: e.activation(x2b[:], LL[:], AF.Identity,
                                             scale=rstd[p], bias=nmean[p]),
          waits=[lm(f"attn_{i}"), lm("ln_chain"), ("sp", CNT["sp"]),
                 ("gp", CNT["gp"])], inc="sp")
        mark(f"ln2aff_{i}", "sp")
        wat = [lm(f"xres_{i}"), lm(f"ln2aff_{i}")]
        if DEBUG:
            wat.append(("qst", CNT["qst"]))
        if i >= 1:
            wat.append(lm(f"y3_{i - 1}"))
        E("v", lambda e, p=p: e.tensor_add(x2b[:], x2b[:], xblk[p][:]),
          waits=wat, inc="vp")
        mark(f"ata2_{i}", "vp")
        if DEBUG:
            E("s", lambda e, i=i: e.dma_start(out=dbg_x2[128 * i:128 * (i + 1), :],
                                              in_=x2b[:]),
              waits=[lm(f"ata2_{i}")], inc="qst", inck=16, selfwait=False)
        emit_ln_stats(p, x2b, [lm(f"ata2_{i}")])
        wy = [lm("ln_chain"), lm(f"ata2_{i}")]
        if i >= 2:
            wy.append(lm(f"y3_used_{i - 2}"))
        else:
            wy.append(lm(f"xn_used_{NBS - 2 + (i & 1)}"))
            wy.append(lm("halo_r_dma") if (i & 1) == 0 else lm("halo_l_dma"))
        E("s", lambda e, p=p: e.activation(xn_bf[p][:], x2b[:], AF.Identity,
                                           scale=rstd[p], bias=nmean[p]),
          waits=wy, inc="sp")
        mark(f"y3_{i}", "sp")
        wtr = [lm(f"y3_{i}")]
        if i >= 2:
            wtr.append(lm(f"y3T_used_{i - 2}"))
        else:
            wtr.append(lm(f"kr_used_{NBS - 2 + (i & 1)}"))
        E("sy", lambda e, p=p: e.dma_start_transpose(
            kr_bf[p].rearrange("pp (b j) -> pp b j", b=8), xn_bf[p][:]),
          waits=wtr, inc="qty", inck=16)
        mark(f"y3T_{i}", "qty")
        mark(f"y3_used_{i}", "qty")
    def emit_qB2(i):
        p = i & 1
        wf = [lm("ffn_built"), lm(f"y3T_{i}")]
        if i == 0:
            wf.append(lm(f"pk_free_{NBS - 1}"))
        else:
            wf.append(lm(f"ata4_{i - 1}"))
        first = True
        for kc in range(9):
            sap = (lambda kc=kc, p=p: ones_blk[:] if kc == 8 else
                   kr_bf[p].rearrange("pp (b j) -> pp b j", b=8)[:, kc, :])()
            for n in range(2):
                isl = (kc == 8 and n == 1)
                E("t", lambda e, sap=sap, kc=kc, n=n: e.matmul(
                    pk[1][:, 512 * n:512 * (n + 1)], sap,
                    wf_bf[:, kc, 512 * n:512 * (n + 1)],
                    start=(kc == 0), stop=(kc == 8), skip_group_check=True),
                  waits=wf if first else (), inc="tp" if isl else None)
                first = False
        mark(f"ffn_{i}", "tp")
        mark(f"y3T_used_{i}", "tp")
        emit_ln_stats(p, pk[1], [lm(f"ffn_{i}")])
        E("s", lambda e, p=p: e.activation(xblk[p][:], pk[1][:], AF.Identity,
                                           scale=rstd[p], bias=nmean[p]),
          waits=[lm(f"ffn_{i}"), lm("ln_chain"), ("sp", CNT["sp"]),
                 ("gp", CNT["gp"])], inc="sp")
        mark(f"ln4aff_{i}", "sp")
        E("v", lambda e, p=p: e.tensor_add(xblk[p][:], xblk[p][:], x2b[:]),
          waits=[lm(f"ln4aff_{i}")], inc="vp")
        mark(f"ata4_{i}", "vp")
        E("s", lambda e, i=i, p=p: e.dma_start(out=out[128 * i:128 * (i + 1), :],
                                               in_=xblk[p][:]),
          waits=[lm(f"ata4_{i}")], inc="qst", inck=16, selfwait=False)
        mark(f"outsb_used_{i}", "qst")

    emit_qA(0)
    emit_qL(0)
    emit_qAT(0)
    emit_xres(0)
    for i in range(1, NBO):
        emit_qA(i)
        if i >= 2:
            emit_qB2(i - 2)
        emit_qB1(i - 1)
        emit_qL(i)
        emit_qAT(i)
        emit_xres(i)
    emit_qB2(NBO - 2)
    emit_qB1(NBO - 1)
    emit_qB2(NBO - 1)


    final_qst = CNT["qst"]
    OPS["s"].append(lambda e: e.wait_ge(SEM["qst"], final_qst))

    with nc.Block() as block:
        @block.vector
        def _(v):
            for f in OPS["v"]:
                f(v)

        @block.scalar
        def _(s):
            for f in OPS["s"]:
                f(s)

        @block.tensor
        def _(t):
            for f in OPS["t"]:
                f(t)

        @block.gpsimd
        def _(g):
            for f in OPS["g"]:
                f(g)

        @block.sync
        def _(sy):
            for f in OPS["sy"]:
                f(sy)

    ctx.close()
    return nc


def _host_prep(L0_attn_x, L0_attn_qk_w, L0_attn_qk_b, L0_attn_lepe_w, L0_attn_lepe_b,
               L0_ffn_weight, L0_ffn_alpha, L0_ffn_bias):
    x = np.ascontiguousarray(np.asarray(L0_attn_x, dtype=np.float32))
    qk_w = np.asarray(L0_attn_qk_w, dtype=np.float32)
    qk_b = np.asarray(L0_attn_qk_b, dtype=np.float32)
    lepe_w = np.asarray(L0_attn_lepe_w, dtype=np.float32)
    lepe_b = np.asarray(L0_attn_lepe_b, dtype=np.float32)
    ffn_w = np.asarray(L0_ffn_weight, dtype=np.float32)
    ffn_a = np.asarray(L0_ffn_alpha, dtype=np.float32)
    ffn_b = np.asarray(L0_ffn_bias, dtype=np.float32)

    perm = np.concatenate([np.arange(0, C, 2), np.arange(1, C, 2)])
    qk_w_p = np.concatenate([qk_w[:, :C][:, perm], qk_w[:, C:][:, perm]], axis=1)
    qk_b_p = np.concatenate([qk_b[:C][perm], qk_b[C:][perm]])

    wqk9 = np.zeros((9, 128, 2 * C), np.float32)
    wqk9[:8] = qk_w_p.reshape(8, 128, 2 * C)
    wqk9[8, 0, :] = qk_b_p

    wlepe9 = np.zeros((3, 9, 128, C), np.float32)
    wlepe9[:, :8] = lepe_w.reshape(3, 8, 128, C)
    wlepe9[1, 8, 0, :] = lepe_b

    wffn9 = np.zeros((9, 128, C), np.float32)
    wffn9[:8] = np.ascontiguousarray(ffn_w.T).reshape(8, 128, C)
    wffn9[8, 0, :] = ffn_b

    alpha_bc = np.ascontiguousarray(np.broadcast_to(ffn_a.reshape(1, 1), (128, 1)),
                                    dtype=np.float32)

    in_maps = []
    for c in range(8):
        b, half = c // 2, c % 2
        pos0 = half * TOWN
        xrot = np.ascontiguousarray(np.concatenate([x[b, pos0:], x[b, :pos0]], axis=0))
        pc = np.empty((128, NBS), np.float32)
        idx = (pos0 + np.arange(NSEQ)) % NSEQ
        pc[:, :] = idx.reshape(NBS, 128).T
        mk = np.zeros((128, 2), np.float32)
        mk[:, 0] = 1.0 if pos0 > 0 else 0.0
        mk[:, 1] = 1.0 if pos0 == 0 else 0.0
        in_maps.append({
            "xseq": xrot, "wqk": wqk9, "wlepe": wlepe9, "wffn": wffn9,
            "alpha": alpha_bc, "poscol": np.ascontiguousarray(pc),
            "masklr": mk,
        })
    return in_maps


def kernel(L0_attn_x, L0_attn_qk_w, L0_attn_qk_b, L0_attn_lepe_w, L0_attn_lepe_b,
           L0_ffn_weight, L0_ffn_alpha, L0_ffn_bias):
    in_maps = _host_prep(L0_attn_x, L0_attn_qk_w, L0_attn_qk_b, L0_attn_lepe_w,
                         L0_attn_lepe_b, L0_ffn_weight, L0_ffn_alpha, L0_ffn_bias)
    nc = build_nc()
    res = run_bass_kernel_spmd(nc, in_maps, core_ids=list(range(8)))
    out = np.empty((4, NSEQ, C), np.float32)
    for c in range(8):
        b, half = c // 2, c % 2
        out[b, half * TOWN:(half + 1) * TOWN] = res.results[c]["out"]
    return out


# version 35
# speedup vs baseline: 1.0189x; 1.0189x over previous
import math
import numpy as np

import concourse.bass as bass
import concourse.mybir as mybir
from concourse.bass_utils import run_bass_kernel_spmd

F32 = mybir.dt.float32
BF16 = mybir.dt.bfloat16
AF = mybir.ActivationFunctionType
ALU = mybir.AluOpType
AX = mybir.AxisListType

C = 1024
H = 8
D = 128
TOWN = 2048
NSEQ = 4096
NBO = TOWN // 128
NBS = NSEQ // 128
NF = 512
EPS = 1e-5

_C = -math.log(10000.0) / 512.0
C_HI = float(np.float32(_C))
C_LO = float(np.float32(_C - C_HI))
TWO_PI = float(np.float32(2 * math.pi))
PI = float(np.float32(math.pi))
THREE_HALF_PI = float(np.float32(3 * math.pi / 2))
INV2PI = float(np.float32(1.0 / (2 * math.pi)))
CW1 = float(np.float32(np.trunc(2 * math.pi * 2**8) / 2**8))
CW2 = float(np.float32(np.trunc((2 * math.pi - CW1) * 2**21) / 2**21))
CW3 = float(np.float32(2 * math.pi - CW1 - CW2))
CW23 = float(np.float32(2 * math.pi - CW1))
HALF_PI = float(np.float32(math.pi / 2))
HP_LO = float(np.float32(math.pi / 2 - HALF_PI))

DEBUG = False
XNT_OFF = 16
COL_LH = 15
COL_RH = XNT_OFF + TOWN
XNT_W = 2080


def build_nc():
    nc = bass.Bass("TRN2", target_bir_lowering=False,
                   detect_race_conditions=False)

    xseq = nc.dram_tensor("xseq", [NSEQ, C], F32, kind="ExternalInput").ap()
    wqk = nc.dram_tensor("wqk", [9, 128, 2 * C], F32, kind="ExternalInput").ap()
    wlepe = nc.dram_tensor("wlepe", [3, 9, 128, C], F32, kind="ExternalInput").ap()
    wffn = nc.dram_tensor("wffn", [9, 128, C], F32, kind="ExternalInput").ap()
    alpha = nc.dram_tensor("alpha", [128, 1], F32, kind="ExternalInput").ap()
    poscol = nc.dram_tensor("poscol", [128, NBS], F32, kind="ExternalInput").ap()
    masklr = nc.dram_tensor("masklr", [128, 2], F32, kind="ExternalInput").ap()
    out = nc.dram_tensor("out", [TOWN, C], F32, kind="ExternalOutput").ap()
    halo_scr = nc.dram_tensor("halo_scr", [C], BF16, kind="Internal").ap()
    if DEBUG:
        dbg_x2 = nc.dram_tensor("dbg_x2", [TOWN, C], F32, kind="ExternalOutput").ap()
        dbg_qz = nc.dram_tensor("dbg_qz", [TOWN, C], F32, kind="ExternalOutput").ap()
        dbg_sm = nc.dram_tensor("dbg_sm", [NBO, 128, 80], F32, kind="ExternalOutput").ap()
        dbg_at = nc.dram_tensor("dbg_at", [TOWN, C], F32, kind="ExternalOutput").ap()

    from contextlib import ExitStack
    ctx = ExitStack()
    sb = lambda name, shp, dt: ctx.enter_context(nc.sbuf_tensor(name, shp, dt))
    psum = lambda name, shp: ctx.enter_context(nc.psum_tensor(name, shp, F32))

    wk_bf = sb("wk_bf", [128, 9, C], BF16)
    wq_bf = sb("wq_bf", [128, 9, C], BF16)
    wl_bf = sb("wl_bf", [128, 25, C], BF16)
    wf_bf = sb("wf_bf", [128, 9, C], BF16)
    xnT = sb("xnT", [128, 8, XNT_W], BF16)
    ones_blk = sb("ones_blk", [128, 128], BF16)
    theta_bc = sb("theta_bc", [128, NF], F32)
    kmb = sb("kmb", [128, C], F32)
    kv_bf = sb("kv_bf", [128, H, D], BF16)
    c256 = sb("c256", [128, NF], F32)
    s256 = sb("s256", [128, NF], F32)
    ones_f32 = sb("ones_f32", [1, 128], F32)
    ones1_bf = sb("ones1_bf", [128, 1], BF16)
    poscol_sb = sb("poscol_sb", [128, NBS], F32)
    masklr_sb = sb("masklr_sb", [128, 2], F32)
    delta_s = sb("delta_s", [1, 1], F32)
    smallf = sb("smallf", [128, 96], F32)
    onesc_f32 = smallf[:, 0:1]
    negpi = smallf[:, 1:2]
    alpha_sb = smallf[:, 2:3]
    delta_bc = smallf[:, 3:4]
    absacc = smallf[:, 4:5]
    absrow = smallf[:, 5:6]
    hpi = smallf[:, 6:7]
    eps_bc = smallf[:, 7:8]

    xblk = [sb(f"xblk{j}", [128, C], F32) for j in range(2)]
    xn_bf = [sb(f"xn_bf{j}", [128, C], BF16) for j in range(3)]
    xnT_tmp = [sb(f"xnT_tmp{j}", [128, 8, 128], BF16) for j in range(2)]
    kh = [sb(f"kh{j}", [128, C], F32) for j in range(2)]
    khb = [sb(f"khb{j}", [128, C], BF16) for j in range(2)]
    kr_bf = [sb(f"krbf{j}", [128, C], BF16) for j in range(2)]
    sinb = [sb(f"sinb{j}", [128, NF], F32) for j in range(2)]
    cosb = [sb(f"cosb{j}", [128, NF], F32) for j in range(2)]
    rm1 = sb("rm1", [128, NF], F32)
    rm2 = sb("rm2", [128, NF], F32)
    rm3 = sb("rm3", [128, NF], F32)
    rm4 = sb("rm4", [128, NF], F32)
    kint = kmb[:, 0:NF].bitcast(mybir.dt.int32)
    qzb = sb("qzb", [128, C], F32)
    angb = qzb[:, 0:NF]
    x2b = sb("x2b", [128, C], F32)
    stage = x2b[:, 0:NF]
    srow = x2b[0:1, 512:1024]
    bnst = [smallf[:, 8 + 12 * j:8 + 12 * j + 12].rearrange("p (a b) -> p a b", a=2)
            for j in range(2)]
    mv = [smallf[:, 32 + 2 * j:34 + 2 * j] for j in range(2)]
    varp = [smallf[:, 36 + j:37 + j] for j in range(2)]
    sqb = [smallf[:, 38 + j:39 + j] for j in range(2)]
    rstd = [smallf[:, 40 + j:41 + j] for j in range(2)]
    nmean = [smallf[:, 42 + j:43 + j] for j in range(2)]
    zden = [smallf[:, 48 + 8 * j:56 + 8 * j] for j in range(2)]
    zval = [smallf[:, 64 + 8 * j:72 + 8 * j] for j in range(2)]

    pk = [psum("pk0", [128, C]), psum("pk1", [128, C])]
    pkv = psum("pkv", [128, C])
    psF = psum("psF", [128, C])

    sem_names = ["qx", "qst", "qwk", "qwq", "qwl", "qc", "qti", "qhl",
                 "qt2", "qt3", "qty", "vp", "sp", "gp", "tp"]
    SEM = {n: ctx.enter_context(nc.semaphore(n)) for n in sem_names}
    CNT = {n: 0 for n in sem_names}
    OPS = {"v": [], "s": [], "t": [], "g": [], "sy": []}
    ENG_SEM = {"v": "vp", "s": "sp", "g": "gp", "t": "tp"}
    RMF = {"w": ("vp", 0)}

    def E(eng, build, waits=(), inc=None, inck=1, selfwait=False):
        wl = []
        for w in waits:
            if isinstance(w, str):
                wl.append((w, CNT[w]))
            else:
                wl.append(w)
        if selfwait and eng in ("v", "s", "g") and CNT[ENG_SEM[eng]] > 0:
            wl.append((ENG_SEM[eng], CNT[ENG_SEM[eng]]))
        if inc is not None:
            CNT[inc] += inck
        here = CNT[inc] if inc is not None else None

        def run(e, _b=build, _w=tuple(wl), _i=inc, _k=inck):
            seen = {}
            for s, v in _w:
                if v > 0 and seen.get(s, -1) < v:
                    seen[s] = v
            for s, v in seen.items():
                e.wait_ge(SEM[s], v)
            ins = _b(e)
            if _i is not None:
                ins.then_inc(SEM[_i], _k)
        OPS[{"v": "v", "s": "s", "g": "g", "t": "t", "sy": "sy"}[eng]].append(run)
        return (inc, here) if inc is not None else None

    L = {}

    def mark(name, sem):
        L[name] = (sem, CNT[sem])

    def lm(name):
        return L[name]

    E("v", lambda e: e.memset(smallf[:], 0.0), inc="vp")
    mark("smallf0", "vp")
    E("sy", lambda e: e.dma_start(out=poscol_sb[:], in_=poscol[:]), inc="qti", inck=16)
    E("sy", lambda e: e.dma_start(out=masklr_sb[:], in_=masklr[:]), inc="qti", inck=16)
    E("sy", lambda e: e.dma_start(out=alpha_sb, in_=alpha[:]), waits=[lm("smallf0")], inc="qti", inck=16)
    mark("tiny_inputs", "qti")

    E("v", lambda e: e.memset(ones_f32[:], 1.0), inc="vp")
    E("v", lambda e: e.memset(onesc_f32, 1.0), inc="vp")
    E("v", lambda e: e.memset(ones1_bf[:], 1.0), inc="vp")
    E("v", lambda e: e.memset(negpi, -PI), inc="vp")
    E("v", lambda e: e.memset(hpi, HALF_PI), inc="vp")
    E("v", lambda e: e.tensor_scalar_add(hpi, hpi, HP_LO),
      waits=[("vp", CNT["vp"])], inc="vp")
    E("v", lambda e: e.memset(eps_bc, EPS), inc="vp")
    E("v", lambda e: e.memset(ones_blk[:], 1.0), inc="vp")
    E("v", lambda e: e.memset(absacc, 0.0), inc="vp")
    mark("consts", "vp")

    kidx = srow[:, 0:NF]
    tmp_r = theta_bc[0:1, 0:NF]
    E("g", lambda e: e.iota(kidx, [[1, NF]], channel_multiplier=0,
                            allow_small_or_imprecise_dtypes=True), inc="gp")
    mark("kidx", "gp")
    E("s", lambda e: e.activation(tmp_r, kidx, AF.Exp, scale=C_HI),
      waits=[lm("kidx")], inc="sp")
    mark("theta_raw", "sp")
    E("v", lambda e: e.tensor_scalar(kidx, kidx, C_LO, 1.0, ALU.mult, ALU.add),
      waits=[lm("kidx"), lm("theta_raw")], inc="vp")
    E("v", lambda e: e.tensor_tensor(tmp_r, tmp_r, kidx, ALU.mult),
      waits=[lm("theta_raw")], inc="vp")
    mark("theta_row", "vp")
    E("t", lambda e: e.matmul(pk[0][:, 0:512], ones_f32[:], tmp_r, start=True, stop=True),
      waits=[lm("theta_row")], inc="tp")
    mark("theta_ps", "tp")
    E("v", lambda e: e.tensor_copy(theta_bc[:], pk[0][:, 0:512].rearrange("p n -> p n")),
      waits=[lm("theta_ps")], inc="vp")
    mark("theta_bc", "vp")
    mark("pk0_free0", "vp")

    for kc in range(9):
        E("g", lambda e, kc=kc: e.dma_start(out=wk_bf[:, kc, :], in_=wqk[kc, :, C:2 * C]),
          inc="qwk", inck=16, selfwait=False)
    mark("wk_loaded", "qwk")

    wjobs = []
    for kc in range(9):
        wjobs.append(("wq", kc))
    for kc in range(8):
        wjobs.append(("fstat", kc))
    def lidx(t, kc):
        return {0: kc, 1: 8 + kc, 2: 17 + kc}[t]
    for t in range(3):
        for kc in range(9 if t == 1 else 8):
            wjobs.append(("lepe", t, kc))

    stage_h = [stage, x2b[:, 512:1024]]
    stage_users = {0: 0, 1: 0}

    def emit_wjob(job, jidx):
        kind = job[0]
        if kind == "wq":
            kc = job[1]
            E("g", lambda e, kc=kc: e.dma_start(out=wq_bf[:, kc, :], in_=wqk[kc, :, 0:C]),
              inc="qwq", inck=16, selfwait=False)
            if kc == 8:
                mark("wq_loaded", "qwq")
        elif kind == "lepe":
            t, kc = job[1], job[2]
            E("g", lambda e, t=t, kc=kc: e.dma_start(out=wl_bf[:, lidx(t, kc), :],
                                                     in_=wlepe[t, kc, :, :]),
              inc="qwl", inck=16, selfwait=False)
            if t == 2 and kc == 7:
                mark("lepe_loaded", "qwl")
        elif kind == "fstat":
            kc = job[1]
            for hf in range(2):
                stg = stage_h[hf]
                E("s", lambda e, kc=kc, hf=hf, stg=stg: e.dma_start(
                    out=stg[:], in_=wffn[kc, :, 512 * hf:512 * (hf + 1)]),
                  waits=[("vp", stage_users[hf])], inc="qc", inck=16, selfwait=False)
                E("v", lambda e, stg=stg: e.tensor_reduce(absrow, stg[:], AX.X, ALU.add,
                                                          apply_absolute_value=True),
                  waits=["qc"], inc="vp")
                E("v", lambda e: e.tensor_add(absacc, absacc, absrow),
                  waits=[("vp", CNT["vp"])], inc="vp")
                mark("fstat_done", "vp")
                stage_users[hf] = CNT["vp"]
    def emit_ln_stats(p, src_ap, waits):
        E("v", lambda e: e.bn_stats(bnst[p][:, 0, :], src_ap[:, 0:512]), waits=waits, inc="vp")
        E("v", lambda e: e.bn_stats(bnst[p][:, 1, :], src_ap[:, 512:1024]), waits=waits, inc="vp")
        E("v", lambda e: e.bn_aggr(mv[p], bnst[p][:]), waits=[("vp", CNT["vp"])], inc="vp")
        mark("lnaggr", "vp")
        E("s", lambda e: e.activation(sqb[p], mv[p][:, 1:2], AF.Ln, bias=eps_bc),
          waits=[lm("lnaggr"), lm("consts"), ("sp", CNT["sp"])], inc="sp")
        E("s", lambda e: e.activation(rstd[p], sqb[p], AF.Exp, scale=-0.5),
          waits=[("sp", CNT["sp"])], inc="sp")
        mark("lnS", "sp")
        E("v", lambda e: e.scalar_tensor_tensor(nmean[p], mv[p][:, 0:1], -1.0,
                                                rstd[p], ALU.mult, ALU.mult),
          waits=[lm("lnS"), ("vp", CNT["vp"])], inc="vp")
        mark("ln_chain", "vp")

    def emit_full_tables(dst_s, dst_c, mark_name, possrc, wextra=()):
        if possrc[0] == "ap":
            ib = possrc[1]
            E("v", lambda e, i=ib: e.tensor_scalar_mul(angb[:], theta_bc[:],
                                                       poscol_sb[:, i:i + 1]),
              waits=[lm("theta_bc"), lm("tiny_inputs"), ("gp", CNT["gp"]),
                     ("sp", CNT["sp"]), ("vp", CNT["vp"])] + list(wextra),
              inc="vp")
        else:
            f = possrc[1]
            E("v", lambda e, f=f: e.tensor_single_scalar(angb[:], theta_bc[:], f,
                                                         ALU.mult),
              waits=[lm("theta_bc"), ("gp", CNT["gp"]), ("sp", CNT["sp"]),
                     ("vp", CNT["vp"])] + list(wextra),
              inc="vp")
        E("v", lambda e: e.tensor_single_scalar(dst_c[:], angb[:], INV2PI, ALU.mult),
          waits=[("vp", CNT["vp"])], inc="vp")
        E("v", lambda e: e.tensor_copy(kint[:], dst_c[:]),
          waits=[("vp", CNT["vp"])], inc="vp")
        E("v", lambda e: e.tensor_copy(rm1[:], kint[:]),
          waits=[("vp", CNT["vp"])], inc="vp")
        E("v", lambda e: e.scalar_tensor_tensor(dst_c[:], rm1[:], -CW1, angb[:],
                                                ALU.mult, ALU.add),
          waits=[("vp", CNT["vp"])], inc="vp")
        E("v", lambda e: e.scalar_tensor_tensor(dst_c[:], rm1[:], -CW2, dst_c[:],
                                                ALU.mult, ALU.add),
          waits=[("vp", CNT["vp"])], inc="vp")
        E("v", lambda e: e.scalar_tensor_tensor(dst_s[:], rm1[:], -CW3, dst_c[:],
                                                ALU.mult, ALU.add),
          waits=[("vp", CNT["vp"])], inc="vp")
        E("v", lambda e: e.tensor_single_scalar(rm2[:], dst_s[:], PI, ALU.is_gt),
          waits=[("vp", CNT["vp"])], inc="vp")
        E("v", lambda e: e.scalar_tensor_tensor(dst_s[:], rm2[:], -CW1, dst_s[:],
                                                ALU.mult, ALU.add),
          waits=[("vp", CNT["vp"])], inc="vp")
        E("v", lambda e: e.scalar_tensor_tensor(dst_s[:], rm2[:], -CW23, dst_s[:],
                                                ALU.mult, ALU.add),
          waits=[("vp", CNT["vp"])], inc="vp")
        E("v", lambda e: e.tensor_scalar(dst_s[:], dst_s[:], -PI, PI, ALU.max, ALU.min),
          waits=[("vp", CNT["vp"])], inc="vp")
        mark("tbl_v", "vp")
        E("s", lambda e: e.activation(dst_c[:], dst_s[:], AF.Abs),
          waits=[lm("tbl_v")], inc="sp")
        E("s", lambda e: e.activation(dst_s[:], dst_s[:], AF.Sin), inc="sp")
        E("s", lambda e: e.activation(dst_c[:], dst_c[:], AF.Sin, bias=hpi, scale=-1.0),
          waits=[lm("consts")], inc="sp")
        mark(mark_name, "sp")
        RMF["w"] = ("vp", CNT["vp"])

    def emit_delta_tables(p, iblk, sp_edge=False, on_g=False):
        sbp, cbp = sinb[p], cosb[p]
        w = [lm(f"tblr_{iblk - 2}"), lm("d256"), RMF["w"]]
        if sp_edge:
            w.append(("sp", CNT["sp"]))
        if on_g:
            wg = w + [("vp", CNT["vp"])]
            E("g", lambda e: e.tensor_tensor(rm1[:], sbp[:], c256[:], ALU.mult),
              waits=wg, inc="gp")
            E("g", lambda e: e.tensor_tensor(rm2[:], cbp[:], s256[:], ALU.mult),
              inc="gp")
            E("g", lambda e: e.tensor_tensor(rm3[:], cbp[:], c256[:], ALU.mult),
              inc="gp")
            E("g", lambda e: e.tensor_tensor(rm4[:], sbp[:], s256[:], ALU.mult),
              inc="gp")
            E("g", lambda e: e.tensor_add(sbp[:], rm1[:], rm2[:]), inc="gp")
            E("g", lambda e: e.tensor_sub(cbp[:], rm3[:], rm4[:]), inc="gp")
            mark(f"tblr_{iblk}", "gp")
            RMF["w"] = ("gp", CNT["gp"])
            return
        E("g", lambda e: e.tensor_tensor(rm1[:], sbp[:], c256[:], ALU.mult),
          waits=w, inc="gp")
        E("g", lambda e: e.tensor_tensor(rm2[:], cbp[:], s256[:], ALU.mult),
          inc="gp")
        mark(f"dlt_{iblk}", "gp")
        E("v", lambda e: e.tensor_tensor(rm3[:], cbp[:], c256[:], ALU.mult),
          waits=w + [("gp", CNT["gp"])], inc="vp")
        E("v", lambda e: e.tensor_tensor(rm4[:], sbp[:], s256[:], ALU.mult),
          inc="vp")
        E("v", lambda e: e.tensor_add(sbp[:], rm1[:], rm2[:]),
          waits=[lm(f"dlt_{iblk}")], inc="vp")
        E("v", lambda e: e.tensor_sub(cbp[:], rm3[:], rm4[:]),
          inc="vp")
        mark(f"tblr_{iblk}", "vp")
        RMF["w"] = ("vp", CNT["vp"])

    def emit_rope(p, src, dst_bf, extra_waits, tag, tblk, sp_edge=False):
        sbp, cbp = sinb[p], cosb[p]
        w = [lm(f"tblr_{tblk}"), RMF["w"]] + list(extra_waits)
        if sp_edge:
            w.append(("sp", CNT["sp"]))
        E("g", lambda e: e.tensor_tensor(rm1[:], cbp[:], src[:, 0:512], ALU.mult),
          waits=w, inc="gp")
        E("g", lambda e: e.tensor_tensor(rm2[:], sbp[:], src[:, 512:1024], ALU.mult),
          waits=w, inc="gp")
        mark(f"rm12_{tag}", "gp")
        E("v", lambda e: e.tensor_tensor(rm3[:], cbp[:], src[:, 512:1024], ALU.mult),
          waits=w, inc="vp")
        E("v", lambda e: e.tensor_tensor(rm4[:], sbp[:], src[:, 0:512], ALU.mult),
          inc="vp")
        mark(f"rm34_{tag}", "vp")
        E("v", lambda e: e.tensor_sub(dst_bf[:, 0:512], rm1[:], rm2[:]),
          waits=[lm(f"rm12_{tag}")], inc="vp")
        E("v", lambda e: e.tensor_add(dst_bf[:, 512:1024], rm3[:], rm4[:]),
          inc="vp")
        RMF["w"] = ("vp", CNT["vp"])

    def emit_ksumkv(i):
        p = i & 1
        for n in range(2):
            E("t", lambda e, n=n, i=i, p=p: e.matmul(
                psF[0:1, 512 * n:512 * (n + 1)], ones1_bf[:],
                khb[p][:, 512 * n:512 * (n + 1)],
                start=(i == 0), stop=(i == NBS - 1), skip_group_check=True),
              waits=[lm(f"khb_{i}"), lm("consts")], inc="tp" if n == 1 else None)
        mark(f"khb_used_{i}", "tp")
        for h in range(H):
            E("t", lambda e, h=h, i=i, p=p, p3=i % 3: e.matmul(
                pkv[:, D * h:D * (h + 1)], kr_bf[p][:, D * h:D * (h + 1)],
                xn_bf[p3][:, D * h:D * (h + 1)],
                start=(i == 0 and h % 4 == 0), stop=(i == NBS - 1 and h == H - 1),
                skip_group_check=True),
              waits=[lm(f"kr_{i}"), lm(f"xn_{i}")] if h == 0 else (),
              inc="tp" if h == H - 1 else None)
        mark(f"kv_{i}", "tp")
        mark(f"kr_used_{i}", "tp")
        mark(f"xn_used_{i}", "tp")

    def emit_xload(i):
        w = [lm(f"xblk_free_{i - 2}")] if i >= 2 else []
        E("sy", lambda e, i=i, p=i & 1: e.dma_start(out=xblk[p][:], in_=xseq[128 * i:128 * (i + 1), :]),
          waits=w, inc="qx", inck=16, selfwait=False)
        mark(f"xload_{i}", "qx")

    def emit_kFx(i):
        p = i & 1
        p3 = i % 3
        emit_ln_stats(p, xblk[p], [lm(f"xload_{i}")])
        wxn = [lm("ln_chain"), lm(f"xload_{i}")]
        if i >= 3:
            wxn.append(lm(f"xn_used_{i - 3}"))
        if i == NBO + 3:
            wxn.append(lm("halo_r_dma"))
        E("s", lambda e: e.activation(xn_bf[p3][:], xblk[p][:], AF.Identity,
                                      scale=rstd[p], bias=nmean[p]),
          waits=wxn, inc="sp")
        mark(f"xn_{i}", "sp")
        mark(f"xblk_free_{i}", "sp")
        if i < NBO:
            dst = xnT[:, :, XNT_OFF + 128 * i: XNT_OFF + 128 * (i + 1)]
        else:
            dst = xnT_tmp[p][:, :, :]
        wtr = [lm(f"xn_{i}")]
        if i >= NBO + 2:
            wtr.append(lm(f"kstat_used_{i - 2}"))
        E("sy", lambda e, dst=dst, p3=p3: e.dma_start_transpose(dst, xn_bf[p3][:]),
          waits=wtr, inc="qt2", inck=16)
        mark(f"xnT_{i}", "qt2")
        if i + 2 < NBS:
            emit_xload(i + 2)

    def emit_kFp(i):
        p = i & 1
        wmm = [lm(f"xnT_{i}"), lm("wk_loaded"), lm("consts")]
        if i == FD + 1:
            wmm.append(lm("pk1_free_delta"))
        if i == 0:
            wmm.append(lm("pk0_free0"))
        if i >= 2:
            wmm.append(lm(f"pk_free_{i - 2}"))
        first = True
        for kc in range(9):
            if kc == 8:
                sap = ones_blk[:]
            elif i < NBO:
                sap = xnT[:, kc, XNT_OFF + 128 * i: XNT_OFF + 128 * (i + 1)]
            else:
                sap = xnT_tmp[p][:, kc, :]
            for n in range(2):
                isl = (kc == 8 and n == 1)
                E("t", lambda e, sap=sap, kc=kc, n=n, p=p: e.matmul(
                    pk[p][:, 512 * n:512 * (n + 1)], sap,
                    wk_bf[:, kc, 512 * n:512 * (n + 1)],
                    start=(kc == 0), stop=(kc == 8), skip_group_check=True),
                  waits=wmm if first else (), inc="tp" if isl else None)
                first = False
        mark(f"kproj_{i}", "tp")
        mark(f"kstat_used_{i}", "tp")
    def emit_kR1(i):
        p = i & 1
        wxm = [lm(f"kproj_{i}")]
        if i >= 2:
            wxm += [lm(f"khb_{i - 2}")]
        E("s", lambda e: e.activation(kh[p][:], pk[p][:], AF.Exp),
          waits=wxm, inc="sp")
        mark(f"kexp_{i}", "sp")
        E("v", lambda e: e.tensor_scalar_min(kh[p][:], kh[p][:], 1.0),
          waits=[lm(f"kexp_{i}")], inc="vp")
        wst = [lm(f"kproj_{i}")]
        if i >= 2:
            wst.append(lm(f"khb_used_{i - 2}"))
        E("v", lambda e: e.scalar_tensor_tensor(khb[p][:], pk[p][:], 0.0, kh[p][:],
                                                ALU.max, ALU.add),
          waits=wst, inc="vp")
        mark(f"kh_{i}", "vp")
        mark(f"khb_{i}", "vp")
        mark(f"pk_free_{i}", "vp")
    def emit_kR2(i):
        p = i & 1
        wrk = [lm(f"khb_{i}")]
        if i >= 2:
            wrk.append(lm(f"kr_used_{i - 2}"))
        emit_rope(p, khb[p], kr_bf[p], wrk, f"k{i}", i)
        mark(f"kr_{i}", "vp")
        if i + 2 < NBS:
            if i + 2 in (NBO, NBO + 1):
                emit_full_tables(sinb[p], cosb[p], f"tblr_{i + 2}", ("ap", i + 2))
            else:
                emit_delta_tables(p, i + 2)
        if i == NBO:
            E("sy", lambda e, p3=i % 3: e.dma_start(out=halo_scr[:], in_=xn_bf[p3][0:1, :]),
              waits=[lm(f"xn_{i}")], inc="qhl", inck=16)
            E("sy", lambda e: e.dma_start(
                out=xnT[:, :, COL_RH:COL_RH + 1],
                in_=halo_scr.rearrange("(b pp) -> pp b", pp=128)[:, :, None],
                allow_slow_non_contiguous=True),
              waits=["qhl"], inc="qhl", inck=16)
            mark("halo_r_dma", "qhl")
            E("v", lambda e: e.tensor_scalar_mul(
                xnT[:, :, COL_RH:COL_RH + 1], xnT[:, :, COL_RH:COL_RH + 1],
                masklr_sb[:, 1:2]),
              waits=[lm("halo_r_dma"), lm("tiny_inputs")], inc="vp")
            mark("halo_r", "vp")
        if i == NBS - 1:
            E("sy", lambda e, p3=i % 3: e.dma_start(out=halo_scr[:], in_=xn_bf[p3][127:128, :]),
              waits=[lm(f"xn_{i}")], inc="qhl", inck=16)
            E("sy", lambda e: e.dma_start(
                out=xnT[:, :, COL_LH:COL_LH + 1],
                in_=halo_scr.rearrange("(b pp) -> pp b", pp=128)[:, :, None],
                allow_slow_non_contiguous=True),
              waits=["qhl"], inc="qhl", inck=16)
            mark("halo_l_dma", "qhl")
            E("v", lambda e: e.tensor_scalar_mul(
                xnT[:, :, COL_LH:COL_LH + 1], xnT[:, :, COL_LH:COL_LH + 1],
                masklr_sb[:, 0:1]),
              waits=[lm("halo_l_dma"), lm("tiny_inputs")], inc="vp")
            mark("halo_l", "vp")
    sgn_c = x2b[:, 512:1024]
    FD = 10

    def emit_fdelta():
        E("t", lambda e: e.matmul(pk[1][0:1, 0:1], absacc, onesc_f32,
                                  start=True, stop=True, skip_group_check=True),
          waits=[lm("fstat_done"), lm("consts"), lm(f"pk_free_{FD - 1}")], inc="tp")
        mark("fdelta_ps", "tp")
        E("v", lambda e: e.tensor_scalar_mul(delta_s[:], pk[1][0:1, 0:1], 0.7 / float(C * C)),
          waits=[lm("fdelta_ps")], inc="vp")
        mark("fdelta_s", "vp")
        E("t", lambda e: e.matmul(pk[1][:, 0:1], ones_f32[:], delta_s[:],
                                  start=True, stop=True, skip_group_check=True),
          waits=[lm("fdelta_s")], inc="tp")
        mark("fdelta_bc", "tp")
        E("v", lambda e: e.tensor_copy(delta_bc, pk[1][:, 0:1]),
          waits=[lm("fdelta_bc")], inc="vp")
        mark("delta_bc", "vp")
        mark("pk1_free_delta", "vp")

    fjobs = list(range(18))
    fb_last = {"v": 0, "s": 0}

    def emit_fjob(j):
        kc, hf = j // 2, j % 2
        sl = np.s_[512 * hf:512 * (hf + 1)]
        wd = [("vp", fb_last["v"]), ("sp", fb_last["s"])]
        if j == 0:
            wd.append(("vp", stage_users[0]))
        E("s", lambda e, kc=kc, hf=hf: e.dma_start(
            out=stage[:], in_=wffn[kc, :, 512 * hf:512 * (hf + 1)]),
          waits=wd, inc="qc", inck=16, selfwait=False)
        mark("fb_dma", "qc")
        if kc < 8:
            wsg = [lm("fb_dma")]
            if j == 0:
                wsg.append(("vp", stage_users[1]))
            E("s", lambda e: e.sign(sgn_c[:], stage[:]), waits=wsg, inc="sp")
            E("s", lambda e: e.activation(stage[:], stage[:], AF.Abs), inc="sp")
            mark("fb_abs", "sp")
            E("v", lambda e: e.tensor_single_scalar(stage[:], stage[:], delta_bc,
                                                    ALU.is_gt),
              waits=[lm("fb_abs"), lm("delta_bc"), ("vp", CNT["vp"])], inc="vp")
            E("v", lambda e, kc=kc, sl=sl: e.scalar_tensor_tensor(
                wf_bf[:, kc, sl], sgn_c[:], alpha_sb, stage[:],
                ALU.mult, ALU.mult),
              waits=[lm("tiny_inputs")], inc="vp")
        else:
            E("v", lambda e, kc=kc, sl=sl: e.tensor_copy(wf_bf[:, kc, sl], stage[:]),
              waits=[lm("fb_dma"), ("vp", CNT["vp"])], inc="vp")
            if hf == 1:
                mark("ffn_built", "vp")
        fb_last["v"] = CNT["vp"]
        fb_last["s"] = CNT["sp"]

    emit_full_tables(s256, c256, "d256", ("imm", 256.0))
    emit_full_tables(sinb[0], cosb[0], "tblr_0", ("ap", 0))
    emit_full_tables(sinb[1], cosb[1], "tblr_1", ("ap", 1))
    emit_xload(0)
    emit_xload(1)
    emit_kFx(0)
    emit_kFx(1)
    emit_kFp(0)
    for i in range(NBS):
        if i >= 1:
            emit_ksumkv(i - 1)
        if i == FD:
            emit_fdelta()
        emit_kR1(i)
        if i + 2 < NBS:
            emit_kFx(i + 2)
        if i + 1 < NBS:
            emit_kFp(i + 1)
        emit_kR2(i)
        for _ in range(2):
            if wjobs:
                emit_wjob(wjobs.pop(0), 0)
        if i >= FD + 1 and fjobs:
            emit_fjob(fjobs.pop(0))
    emit_ksumkv(NBS - 1)

    while wjobs:
        emit_wjob(wjobs.pop(0), 0)

    E("s", lambda e: e.mul(kv_bf[:].rearrange("p h d -> p (h d)"), pkv[:], 1.0 / 64.0),
      waits=[lm(f"kv_{NBS - 1}")], inc="sp")
    mark("kv_bf", "sp")
    emit_full_tables(sinb[0], cosb[0], "tblr_0", ("ap", 0))
    emit_full_tables(sinb[1], cosb[1], "tblr_1", ("ap", 1))
    for n in range(2):
        E("v", lambda e, n=n: e.tensor_scalar_mul(srow[:], psF[0:1, 512 * n:512 * (n + 1)],
                                                  1.0 / float(NSEQ)),
          waits=[lm(f"khb_used_{NBS - 1}"), ("tp", CNT["tp"]), ("vp", CNT["vp"])], inc="vp")
        mark("kmean_row", "vp")
        E("t", lambda e, n=n: e.matmul(pk[0][:, 512 * n:512 * (n + 1)], ones_f32[:],
                                       srow[:], start=True, stop=True),
          waits=[lm("kmean_row"), lm(f"pk_free_{NBS - 2}")], inc="tp")
    mark("psF_free_boot", "vp")
    mark("kmb_ps", "tp")
    E("v", lambda e: e.tensor_copy(kmb[:], pk[0][:]), waits=[lm("kmb_ps")], inc="vp")
    mark("kmb", "vp")
    mark("pk_free_boot0", "vp")

    def emit_xres(i):
        w = [("qst", CNT["qst"])]
        if i < 2:
            w.append(lm(f"xblk_free_{NBS - 2 + i}"))
        E("sy", lambda e, i=i, p=i & 1: e.dma_start(out=xblk[p][:], in_=xseq[128 * i:128 * (i + 1), :]),
          waits=w, inc="qx", inck=16, selfwait=False)
        mark(f"xres_{i}", "qx")

    def emit_qA(i):
        p = i & 1
        wmm = [lm("wq_loaded")]
        if i == 0:
            wmm.append(lm("pk_free_boot0"))
        else:
            wmm.append(lm(f"qpk_free_{i - 1}"))
        first = True
        for kc in range(9):
            sap = (lambda kc=kc, i=i: ones_blk[:] if kc == 8 else
                   xnT[:, kc, XNT_OFF + 128 * i: XNT_OFF + 128 * (i + 1)])()
            for n in range(2):
                isl = (kc == 8 and n == 1)
                E("t", lambda e, sap=sap, kc=kc, n=n: e.matmul(
                    pk[0][:, 512 * n:512 * (n + 1)], sap,
                    wq_bf[:, kc, 512 * n:512 * (n + 1)],
                    start=(kc == 0), stop=(kc == 8)),
                  waits=wmm if first else (), inc="tp" if isl else None)
                first = False
        mark(f"qproj_{i}", "tp")
        wxm = [lm(f"qproj_{i}")]
        if i >= 2:
            wxm += [lm(f"qh_{i - 2}"), lm(f"rm34_q{i - 2}")]
        else:
            wxm += [lm(f"khb_{NBS - 2 + (i & 1)}")]
        E("s", lambda e: e.activation(kh[p][:], pk[0][:], AF.Exp),
          waits=wxm, inc="sp")
        mark(f"qexp_{i}", "sp")
        E("v", lambda e: e.tensor_scalar_min(kh[p][:], kh[p][:], 1.0),
          waits=[lm(f"qexp_{i}")], inc="vp")
        E("v", lambda e: e.scalar_tensor_tensor(kh[p][:], pk[0][:], 0.0, kh[p][:],
                                                ALU.max, ALU.add),
          waits=[lm(f"qproj_{i}")], inc="vp")
        mark(f"qh_{i}", "vp")
        mark(f"qpk_free_{i}", "vp")
        E("v", lambda e: e.tensor_tensor(qzb[:], kh[p][:], kmb[:], ALU.mult),
          waits=[lm("kmb"), lm(f"qh_{i}"), ("qst", CNT["qst"])], inc="vp")
        mark(f"prod_{i}", "vp")
        E("v", lambda e: e.tensor_reduce(
            zden[p], qzb.rearrange("pp (s h j) -> pp h s j", s=2, h=H, j=64),
            AX.XY, ALU.add), waits=[lm(f"prod_{i}"), ("vp", CNT["vp"])], inc="vp")
        E("v", lambda e: e.tensor_scalar_add(zden[p], zden[p], 1e-6),
          waits=[("vp", CNT["vp"])], inc="vp")
        E("v", lambda e: e.reciprocal(zval[p], zden[p]),
          waits=[("vp", CNT["vp"])], inc="vp")
        mark(f"z_{i}", "vp")
        if DEBUG:
            E("s", lambda e, i=i: e.dma_start(out=dbg_sm[i], in_=smallf[:, 0:80]),
              waits=[lm(f"z_{i}")], inc="qst", inck=16, selfwait=False)
        wr = [lm(f"qh_{i}")]
        if i == 0:
            wr.append(lm(f"khb_used_{NBS - 1}"))
        emit_rope(p, kh[p], qzb, wr, f"q{i}", i, sp_edge=(i < 2))
        mark(f"qr_{i}", "vp")
        wqz = [lm(f"qr_{i}"), lm(f"z_{i}")]
        if i >= 2:
            wqz.append(lm(f"qrz_used_{i - 2}"))
        else:
            wqz.append(lm(f"khb_used_{NBS - 2 + (i & 1)}"))
        E("v", lambda e, p=p: e.tensor_tensor(
            khb[p].rearrange("pp (h j) -> pp h j", h=H),
            qzb.rearrange("pp (h j) -> pp h j", h=H),
            zval[p][:, :, None].broadcast_to((128, H, D)), ALU.mult),
          waits=wqz, inc="vp")
        mark(f"qrz_{i}", "vp")
        if DEBUG:
            E("s", lambda e, i=i: e.dma_start(out=dbg_qz[128 * i:128 * (i + 1), :],
                                              in_=qzb[:]),
              waits=[lm(f"qr_{i}")], inc="qst", inck=16, selfwait=False)
        wtr = [lm(f"qrz_{i}")]
        if i >= 2:
            wtr.append(lm(f"qrT_used_{i - 2}"))
        else:
            wtr.append(lm(f"kstat_used_{NBS - 2 + (i & 1)}"))
        E("sy", lambda e, p=p: e.dma_start_transpose(xnT_tmp[p][:, :, :], khb[p][:]),
          waits=wtr, inc="qt3", inck=16)
        mark(f"qrT_{i}", "qt3")
        mark(f"qrz_used_{i}", "qt3")
        if i + 2 < NBO:
            emit_delta_tables(p, i + 2, sp_edge=(i < 2), on_g=True)
    def emit_qL(i):
        p = i & 1
        LL = [pkv, psF][p]
        wl = [lm("lepe_loaded"), lm("halo_l"), lm("halo_r")]
        if i == 0:
            wl.append(lm("kv_bf"))
        elif i == 1:
            wl.append(lm("psF_free_boot"))
        if i >= 2:
            wl.append(lm(f"ata2_{i - 2}"))
        first = True
        for t in range(3):
            for kc in range(9):
                if kc == 8 and t != 1:
                    continue
                sap = (lambda t=t, kc=kc, i=i: ones_blk[:] if kc == 8 else
                       xnT[:, kc, XNT_OFF + 128 * i + t - 1: XNT_OFF + 128 * i + t - 1 + 128])()
                for n in range(2):
                    isl = (t == 2 and kc == 7 and n == 1)
                    E("t", lambda e, sap=sap, li=lidx(t, kc), t=t, kc=kc, n=n, LL=LL: e.matmul(
                        LL[:, 512 * n:512 * (n + 1)], sap,
                        wl_bf[:, li, 512 * n:512 * (n + 1)],
                        start=(t == 0 and kc == 0), stop=False,
                        skip_group_check=True),
                      waits=wl if first else (), inc="tp" if isl else None)
                    first = False
        mark(f"lepe_{i}", "tp")
    def emit_qAT(i):
        p = i & 1
        LL = [pkv, psF][p]
        for h in range(H):
            E("t", lambda e, h=h, p=p, LL=LL: e.matmul(
                LL[:, D * h:D * (h + 1)], xnT_tmp[p][:, h, :], kv_bf[:, h, :],
                start=False, stop=(h == H - 1), skip_group_check=True),
              waits=[lm(f"qrT_{i}"), lm("kv_bf"), lm(f"lepe_{i}")] if h == 0 else (),
              inc="tp" if h == H - 1 else None)
        mark(f"attn_{i}", "tp")
        mark(f"qrT_used_{i}", "tp")
    def emit_qB1(i):
        p = i & 1
        LL = [pkv, psF][p]
        if DEBUG:
            E("v", lambda e, LL=LL: e.tensor_copy(x2b[:], LL[:]),
              waits=[lm(f"attn_{i}"), ("qst", CNT["qst"]), ("vp", CNT["vp"])], inc="vp")
            mark(f"dbgat_{i}", "vp")
            E("s", lambda e, i=i: e.dma_start(out=dbg_at[128 * i:128 * (i + 1), :],
                                              in_=x2b[:]),
              waits=[lm(f"dbgat_{i}")], inc="qst", inck=16, selfwait=False)
        emit_ln_stats(p, LL, [lm(f"attn_{i}")])
        E("s", lambda e, LL=LL: e.activation(x2b[:], LL[:], AF.Identity,
                                             scale=rstd[p], bias=nmean[p]),
          waits=[lm(f"attn_{i}"), lm("ln_chain"), ("sp", CNT["sp"]),
                 ("gp", CNT["gp"])], inc="sp")
        mark(f"ln2aff_{i}", "sp")
        wat = [lm(f"xres_{i}"), lm(f"ln2aff_{i}")]
        if DEBUG:
            wat.append(("qst", CNT["qst"]))
        if i >= 1:
            wat.append(lm(f"y3_{i - 1}"))
        E("v", lambda e, p=p: e.tensor_add(x2b[:], x2b[:], xblk[p][:]),
          waits=wat, inc="vp")
        mark(f"ata2_{i}", "vp")
        if DEBUG:
            E("s", lambda e, i=i: e.dma_start(out=dbg_x2[128 * i:128 * (i + 1), :],
                                              in_=x2b[:]),
              waits=[lm(f"ata2_{i}")], inc="qst", inck=16, selfwait=False)
        emit_ln_stats(p, x2b, [lm(f"ata2_{i}")])
        wy = [lm("ln_chain"), lm(f"ata2_{i}")]
        if i >= 2:
            wy.append(lm(f"y3_used_{i - 2}"))
        else:
            wy.append(lm(f"xn_used_{NBS - 2 + (i & 1)}"))
            wy.append(lm("halo_r_dma") if (i & 1) == 0 else lm("halo_l_dma"))
        E("s", lambda e, p=p: e.activation(xn_bf[p][:], x2b[:], AF.Identity,
                                           scale=rstd[p], bias=nmean[p]),
          waits=wy, inc="sp")
        mark(f"y3_{i}", "sp")
        wtr = [lm(f"y3_{i}")]
        if i >= 2:
            wtr.append(lm(f"y3T_used_{i - 2}"))
        else:
            wtr.append(lm(f"kr_used_{NBS - 2 + (i & 1)}"))
        E("sy", lambda e, p=p: e.dma_start_transpose(
            kr_bf[p].rearrange("pp (b j) -> pp b j", b=8), xn_bf[p][:]),
          waits=wtr, inc="qty", inck=16)
        mark(f"y3T_{i}", "qty")
        mark(f"y3_used_{i}", "qty")
    def emit_qB2(i):
        p = i & 1
        wf = [lm("ffn_built"), lm(f"y3T_{i}")]
        if i == 0:
            wf.append(lm(f"pk_free_{NBS - 1}"))
        else:
            wf.append(lm(f"ata4_{i - 1}"))
        first = True
        for kc in range(9):
            sap = (lambda kc=kc, p=p: ones_blk[:] if kc == 8 else
                   kr_bf[p].rearrange("pp (b j) -> pp b j", b=8)[:, kc, :])()
            for n in range(2):
                isl = (kc == 8 and n == 1)
                E("t", lambda e, sap=sap, kc=kc, n=n: e.matmul(
                    pk[1][:, 512 * n:512 * (n + 1)], sap,
                    wf_bf[:, kc, 512 * n:512 * (n + 1)],
                    start=(kc == 0), stop=(kc == 8), skip_group_check=True),
                  waits=wf if first else (), inc="tp" if isl else None)
                first = False
        mark(f"ffn_{i}", "tp")
        mark(f"y3T_used_{i}", "tp")
        emit_ln_stats(p, pk[1], [lm(f"ffn_{i}")])
        E("s", lambda e, p=p: e.activation(xblk[p][:], pk[1][:], AF.Identity,
                                           scale=rstd[p], bias=nmean[p]),
          waits=[lm(f"ffn_{i}"), lm("ln_chain"), ("sp", CNT["sp"]),
                 ("gp", CNT["gp"])], inc="sp")
        mark(f"ln4aff_{i}", "sp")
        E("v", lambda e, p=p: e.tensor_add(xblk[p][:], xblk[p][:], x2b[:]),
          waits=[lm(f"ln4aff_{i}")], inc="vp")
        mark(f"ata4_{i}", "vp")
        E("s", lambda e, i=i, p=p: e.dma_start(out=out[128 * i:128 * (i + 1), :],
                                               in_=xblk[p][:]),
          waits=[lm(f"ata4_{i}")], inc="qst", inck=16, selfwait=False)
        mark(f"outsb_used_{i}", "qst")

    emit_qA(0)
    emit_qL(0)
    emit_qAT(0)
    emit_xres(0)
    for i in range(1, NBO):
        emit_qA(i)
        if i >= 2:
            emit_qB2(i - 2)
        emit_qB1(i - 1)
        emit_qL(i)
        emit_qAT(i)
        emit_xres(i)
    emit_qB2(NBO - 2)
    emit_qB1(NBO - 1)
    emit_qB2(NBO - 1)


    final_qst = CNT["qst"]
    OPS["s"].append(lambda e: e.wait_ge(SEM["qst"], final_qst))

    with nc.Block() as block:
        @block.vector
        def _(v):
            for f in OPS["v"]:
                f(v)

        @block.scalar
        def _(s):
            for f in OPS["s"]:
                f(s)

        @block.tensor
        def _(t):
            for f in OPS["t"]:
                f(t)

        @block.gpsimd
        def _(g):
            for f in OPS["g"]:
                f(g)

        @block.sync
        def _(sy):
            for f in OPS["sy"]:
                f(sy)

    ctx.close()
    return nc


def _host_prep(L0_attn_x, L0_attn_qk_w, L0_attn_qk_b, L0_attn_lepe_w, L0_attn_lepe_b,
               L0_ffn_weight, L0_ffn_alpha, L0_ffn_bias):
    x = np.ascontiguousarray(np.asarray(L0_attn_x, dtype=np.float32))
    qk_w = np.asarray(L0_attn_qk_w, dtype=np.float32)
    qk_b = np.asarray(L0_attn_qk_b, dtype=np.float32)
    lepe_w = np.asarray(L0_attn_lepe_w, dtype=np.float32)
    lepe_b = np.asarray(L0_attn_lepe_b, dtype=np.float32)
    ffn_w = np.asarray(L0_ffn_weight, dtype=np.float32)
    ffn_a = np.asarray(L0_ffn_alpha, dtype=np.float32)
    ffn_b = np.asarray(L0_ffn_bias, dtype=np.float32)

    perm = np.concatenate([np.arange(0, C, 2), np.arange(1, C, 2)])
    qk_w_p = np.concatenate([qk_w[:, :C][:, perm], qk_w[:, C:][:, perm]], axis=1)
    qk_b_p = np.concatenate([qk_b[:C][perm], qk_b[C:][perm]])

    wqk9 = np.zeros((9, 128, 2 * C), np.float32)
    wqk9[:8] = qk_w_p.reshape(8, 128, 2 * C)
    wqk9[8, 0, :] = qk_b_p

    wlepe9 = np.zeros((3, 9, 128, C), np.float32)
    wlepe9[:, :8] = lepe_w.reshape(3, 8, 128, C)
    wlepe9[1, 8, 0, :] = lepe_b

    wffn9 = np.zeros((9, 128, C), np.float32)
    wffn9[:8] = np.ascontiguousarray(ffn_w.T).reshape(8, 128, C)
    wffn9[8, 0, :] = ffn_b

    alpha_bc = np.ascontiguousarray(np.broadcast_to(ffn_a.reshape(1, 1), (128, 1)),
                                    dtype=np.float32)

    in_maps = []
    for c in range(8):
        b, half = c // 2, c % 2
        pos0 = half * TOWN
        xrot = np.ascontiguousarray(np.concatenate([x[b, pos0:], x[b, :pos0]], axis=0))
        pc = np.empty((128, NBS), np.float32)
        idx = (pos0 + np.arange(NSEQ)) % NSEQ
        pc[:, :] = idx.reshape(NBS, 128).T
        mk = np.zeros((128, 2), np.float32)
        mk[:, 0] = 1.0 if pos0 > 0 else 0.0
        mk[:, 1] = 1.0 if pos0 == 0 else 0.0
        in_maps.append({
            "xseq": xrot, "wqk": wqk9, "wlepe": wlepe9, "wffn": wffn9,
            "alpha": alpha_bc, "poscol": np.ascontiguousarray(pc),
            "masklr": mk,
        })
    return in_maps


def kernel(L0_attn_x, L0_attn_qk_w, L0_attn_qk_b, L0_attn_lepe_w, L0_attn_lepe_b,
           L0_ffn_weight, L0_ffn_alpha, L0_ffn_bias):
    in_maps = _host_prep(L0_attn_x, L0_attn_qk_w, L0_attn_qk_b, L0_attn_lepe_w,
                         L0_attn_lepe_b, L0_ffn_weight, L0_ffn_alpha, L0_ffn_bias)
    nc = build_nc()
    res = run_bass_kernel_spmd(nc, in_maps, core_ids=list(range(8)))
    out = np.empty((4, NSEQ, C), np.float32)
    for c in range(8):
        b, half = c // 2, c % 2
        out[b, half * TOWN:(half + 1) * TOWN] = res.results[c]["out"]
    return out


# version 36 (final)
# speedup vs baseline: 1.0340x; 1.0148x over previous
import math
import numpy as np

import concourse.bass as bass
import concourse.mybir as mybir
from concourse.bass_utils import run_bass_kernel_spmd

F32 = mybir.dt.float32
BF16 = mybir.dt.bfloat16
AF = mybir.ActivationFunctionType
ALU = mybir.AluOpType
AX = mybir.AxisListType

C = 1024
H = 8
D = 128
TOWN = 2048
NSEQ = 4096
NBO = TOWN // 128
NBS = NSEQ // 128
NF = 512
EPS = 1e-5

_C = -math.log(10000.0) / 512.0
C_HI = float(np.float32(_C))
C_LO = float(np.float32(_C - C_HI))
TWO_PI = float(np.float32(2 * math.pi))
PI = float(np.float32(math.pi))
THREE_HALF_PI = float(np.float32(3 * math.pi / 2))
INV2PI = float(np.float32(1.0 / (2 * math.pi)))
CW1 = float(np.float32(np.trunc(2 * math.pi * 2**8) / 2**8))
CW2 = float(np.float32(np.trunc((2 * math.pi - CW1) * 2**21) / 2**21))
CW3 = float(np.float32(2 * math.pi - CW1 - CW2))
CW23 = float(np.float32(2 * math.pi - CW1))
HALF_PI = float(np.float32(math.pi / 2))
HP_LO = float(np.float32(math.pi / 2 - HALF_PI))

DEBUG = False
XNT_OFF = 16
COL_LH = 15
COL_RH = XNT_OFF + TOWN
XNT_W = 2080


def build_nc():
    nc = bass.Bass("TRN2", target_bir_lowering=False,
                   detect_race_conditions=False)

    xseq = nc.dram_tensor("xseq", [NSEQ, C], F32, kind="ExternalInput").ap()
    wqk = nc.dram_tensor("wqk", [9, 128, 2 * C], F32, kind="ExternalInput").ap()
    wlepe = nc.dram_tensor("wlepe", [3, 9, 128, C], F32, kind="ExternalInput").ap()
    wffn = nc.dram_tensor("wffn", [9, 128, C], F32, kind="ExternalInput").ap()
    alpha = nc.dram_tensor("alpha", [128, 1], F32, kind="ExternalInput").ap()
    poscol = nc.dram_tensor("poscol", [128, NBS], F32, kind="ExternalInput").ap()
    masklr = nc.dram_tensor("masklr", [128, 2], F32, kind="ExternalInput").ap()
    out = nc.dram_tensor("out", [TOWN, C], F32, kind="ExternalOutput").ap()
    halo_scr = nc.dram_tensor("halo_scr", [C], BF16, kind="Internal").ap()
    if DEBUG:
        dbg_x2 = nc.dram_tensor("dbg_x2", [TOWN, C], F32, kind="ExternalOutput").ap()
        dbg_qz = nc.dram_tensor("dbg_qz", [TOWN, C], F32, kind="ExternalOutput").ap()
        dbg_sm = nc.dram_tensor("dbg_sm", [NBO, 128, 80], F32, kind="ExternalOutput").ap()
        dbg_at = nc.dram_tensor("dbg_at", [TOWN, C], F32, kind="ExternalOutput").ap()

    from contextlib import ExitStack
    ctx = ExitStack()
    sb = lambda name, shp, dt: ctx.enter_context(nc.sbuf_tensor(name, shp, dt))
    psum = lambda name, shp: ctx.enter_context(nc.psum_tensor(name, shp, F32))

    wk_bf = sb("wk_bf", [128, 9, C], BF16)
    wq_bf = sb("wq_bf", [128, 9, C], BF16)
    wl_bf = sb("wl_bf", [128, 25, C], BF16)
    wf_bf = sb("wf_bf", [128, 9, C], BF16)
    xnT = sb("xnT", [128, 8, XNT_W], BF16)
    ones_blk = sb("ones_blk", [128, 128], BF16)
    theta_bc = sb("theta_bc", [128, NF], F32)
    kmb = sb("kmb", [128, C], F32)
    kv_bf = sb("kv_bf", [128, H, D], BF16)
    c256 = sb("c256", [128, NF], F32)
    s256 = sb("s256", [128, NF], F32)
    ones_f32 = sb("ones_f32", [1, 128], F32)
    ones1_bf = sb("ones1_bf", [128, 1], BF16)
    poscol_sb = sb("poscol_sb", [128, NBS], F32)
    masklr_sb = sb("masklr_sb", [128, 2], F32)
    delta_s = sb("delta_s", [1, 1], F32)
    smallf = sb("smallf", [128, 96], F32)
    onesc_f32 = smallf[:, 0:1]
    negpi = smallf[:, 1:2]
    alpha_sb = smallf[:, 2:3]
    delta_bc = smallf[:, 3:4]
    absacc = smallf[:, 4:5]
    absrow = smallf[:, 5:6]
    hpi = smallf[:, 6:7]
    eps_bc = smallf[:, 7:8]

    xblk = [sb(f"xblk{j}", [128, C], F32) for j in range(2)]
    xn_bf = [sb(f"xn_bf{j}", [128, C], BF16) for j in range(3)]
    xnT_tmp = [sb(f"xnT_tmp{j}", [128, 8, 128], BF16) for j in range(2)]
    kh = [sb(f"kh{j}", [128, C], F32) for j in range(2)]
    khb = [sb(f"khb{j}", [128, C], BF16) for j in range(2)]
    kr_bf = [sb(f"krbf{j}", [128, C], BF16) for j in range(2)]
    sinb = [sb(f"sinb{j}", [128, NF], F32) for j in range(2)]
    cosb = [sb(f"cosb{j}", [128, NF], F32) for j in range(2)]
    rm1 = sb("rm1", [128, NF], F32)
    rm2 = sb("rm2", [128, NF], F32)
    rm3 = sb("rm3", [128, NF], F32)
    rm4 = sb("rm4", [128, NF], F32)
    kint = kmb[:, 0:NF].bitcast(mybir.dt.int32)
    qzb = sb("qzb", [128, C], F32)
    angb = qzb[:, 0:NF]
    x2b = sb("x2b", [128, C], F32)
    stage = x2b[:, 0:NF]
    srow = x2b[0:1, 512:1024]
    bnst = [smallf[:, 8 + 12 * j:8 + 12 * j + 12].rearrange("p (a b) -> p a b", a=2)
            for j in range(2)]
    mv = [smallf[:, 32 + 2 * j:34 + 2 * j] for j in range(2)]
    varp = [smallf[:, 36 + j:37 + j] for j in range(2)]
    sqb = [smallf[:, 38 + j:39 + j] for j in range(2)]
    rstd = [smallf[:, 40 + j:41 + j] for j in range(2)]
    nmean = [smallf[:, 42 + j:43 + j] for j in range(2)]
    zden = [smallf[:, 48 + 8 * j:56 + 8 * j] for j in range(2)]
    zval = [smallf[:, 64 + 8 * j:72 + 8 * j] for j in range(2)]

    pk = [psum("pk0", [128, C]), psum("pk1", [128, C])]
    pkv = psum("pkv", [128, C])
    psF = psum("psF", [128, C])

    sem_names = ["qx", "qst", "qwk", "qwq", "qwl", "qc", "qti", "qhl",
                 "qt2", "qt3", "qty", "vp", "sp", "gp", "tp"]
    SEM = {n: ctx.enter_context(nc.semaphore(n)) for n in sem_names}
    CNT = {n: 0 for n in sem_names}
    OPS = {"v": [], "s": [], "t": [], "g": [], "sy": []}
    ENG_SEM = {"v": "vp", "s": "sp", "g": "gp", "t": "tp"}
    RMF = {"w": ("vp", 0)}

    def E(eng, build, waits=(), inc=None, inck=1, selfwait=False):
        wl = []
        for w in waits:
            if isinstance(w, str):
                wl.append((w, CNT[w]))
            else:
                wl.append(w)
        if selfwait and eng in ("v", "s", "g") and CNT[ENG_SEM[eng]] > 0:
            wl.append((ENG_SEM[eng], CNT[ENG_SEM[eng]]))
        if inc is not None:
            CNT[inc] += inck
        here = CNT[inc] if inc is not None else None

        def run(e, _b=build, _w=tuple(wl), _i=inc, _k=inck):
            seen = {}
            for s, v in _w:
                if v > 0 and seen.get(s, -1) < v:
                    seen[s] = v
            for s, v in seen.items():
                e.wait_ge(SEM[s], v)
            ins = _b(e)
            if _i is not None:
                ins.then_inc(SEM[_i], _k)
        OPS[{"v": "v", "s": "s", "g": "g", "t": "t", "sy": "sy"}[eng]].append(run)
        return (inc, here) if inc is not None else None

    L = {}

    def mark(name, sem):
        L[name] = (sem, CNT[sem])

    def lm(name):
        return L[name]

    E("v", lambda e: e.memset(smallf[:], 0.0), inc="vp")
    mark("smallf0", "vp")
    E("sy", lambda e: e.dma_start(out=poscol_sb[:], in_=poscol[:]), inc="qti", inck=16)
    E("sy", lambda e: e.dma_start(out=masklr_sb[:], in_=masklr[:]), inc="qti", inck=16)
    E("sy", lambda e: e.dma_start(out=alpha_sb, in_=alpha[:]), waits=[lm("smallf0")], inc="qti", inck=16)
    mark("tiny_inputs", "qti")

    E("v", lambda e: e.memset(ones_f32[:], 1.0), inc="vp")
    E("v", lambda e: e.memset(onesc_f32, 1.0), inc="vp")
    E("v", lambda e: e.memset(ones1_bf[:], 1.0), inc="vp")
    E("v", lambda e: e.memset(negpi, -PI), inc="vp")
    E("v", lambda e: e.memset(hpi, HALF_PI), inc="vp")
    E("v", lambda e: e.tensor_scalar_add(hpi, hpi, HP_LO),
      waits=[("vp", CNT["vp"])], inc="vp")
    E("v", lambda e: e.memset(eps_bc, EPS), inc="vp")
    E("v", lambda e: e.memset(ones_blk[:], 1.0), inc="vp")
    E("v", lambda e: e.memset(absacc, 0.0), inc="vp")
    mark("consts", "vp")

    kidx = srow[:, 0:NF]
    tmp_r = theta_bc[0:1, 0:NF]
    E("g", lambda e: e.iota(kidx, [[1, NF]], channel_multiplier=0,
                            allow_small_or_imprecise_dtypes=True), inc="gp")
    mark("kidx", "gp")
    E("s", lambda e: e.activation(tmp_r, kidx, AF.Exp, scale=C_HI),
      waits=[lm("kidx")], inc="sp")
    mark("theta_raw", "sp")
    E("v", lambda e: e.tensor_scalar(kidx, kidx, C_LO, 1.0, ALU.mult, ALU.add),
      waits=[lm("kidx"), lm("theta_raw")], inc="vp")
    E("v", lambda e: e.tensor_tensor(tmp_r, tmp_r, kidx, ALU.mult),
      waits=[lm("theta_raw")], inc="vp")
    mark("theta_row", "vp")
    E("t", lambda e: e.matmul(pk[0][:, 0:512], ones_f32[:], tmp_r, start=True, stop=True),
      waits=[lm("theta_row")], inc="tp")
    mark("theta_ps", "tp")
    E("v", lambda e: e.tensor_copy(theta_bc[:], pk[0][:, 0:512].rearrange("p n -> p n")),
      waits=[lm("theta_ps")], inc="vp")
    mark("theta_bc", "vp")
    mark("pk0_free0", "vp")

    for kc in range(9):
        E("g", lambda e, kc=kc: e.dma_start(out=wk_bf[:, kc, :], in_=wqk[kc, :, C:2 * C]),
          inc="qwk", inck=16, selfwait=False)
    mark("wk_loaded", "qwk")

    wjobs = []
    for kc in range(9):
        wjobs.append(("wq", kc))
    for kc in range(8):
        wjobs.append(("fstat", kc))
    def lidx(t, kc):
        return {0: kc, 1: 8 + kc, 2: 17 + kc}[t]
    for t in range(3):
        for kc in range(9 if t == 1 else 8):
            wjobs.append(("lepe", t, kc))

    stage_h = [stage, x2b[:, 512:1024]]
    stage_users = {0: 0, 1: 0}

    def emit_wjob(job, jidx):
        kind = job[0]
        if kind == "wq":
            kc = job[1]
            E("g", lambda e, kc=kc: e.dma_start(out=wq_bf[:, kc, :], in_=wqk[kc, :, 0:C]),
              inc="qwq", inck=16, selfwait=False)
            if kc == 8:
                mark("wq_loaded", "qwq")
        elif kind == "lepe":
            t, kc = job[1], job[2]
            E("g", lambda e, t=t, kc=kc: e.dma_start(out=wl_bf[:, lidx(t, kc), :],
                                                     in_=wlepe[t, kc, :, :]),
              inc="qwl", inck=16, selfwait=False)
            if t == 2 and kc == 7:
                mark("lepe_loaded", "qwl")
        elif kind == "fstat":
            kc = job[1]
            for hf in range(2):
                stg = stage_h[hf]
                E("s", lambda e, kc=kc, hf=hf, stg=stg: e.dma_start(
                    out=stg[:], in_=wffn[kc, :, 512 * hf:512 * (hf + 1)]),
                  waits=[("vp", stage_users[hf])], inc="qc", inck=16, selfwait=False)
                E("v", lambda e, stg=stg: e.tensor_reduce(absrow, stg[:], AX.X, ALU.add,
                                                          apply_absolute_value=True),
                  waits=["qc"], inc="vp")
                E("v", lambda e: e.tensor_add(absacc, absacc, absrow),
                  waits=[("vp", CNT["vp"])], inc="vp")
                mark("fstat_done", "vp")
                stage_users[hf] = CNT["vp"]
    def emit_ln_stats(p, src_ap, waits):
        E("v", lambda e: e.bn_stats(bnst[p][:, 0, :], src_ap[:, 0:512]), waits=waits, inc="vp")
        E("v", lambda e: e.bn_stats(bnst[p][:, 1, :], src_ap[:, 512:1024]), waits=waits, inc="vp")
        E("v", lambda e: e.bn_aggr(mv[p], bnst[p][:]), waits=[("vp", CNT["vp"])], inc="vp")
        mark("lnaggr", "vp")
        E("s", lambda e: e.activation(sqb[p], mv[p][:, 1:2], AF.Ln, bias=eps_bc),
          waits=[lm("lnaggr"), lm("consts"), ("sp", CNT["sp"])], inc="sp")
        E("s", lambda e: e.activation(rstd[p], sqb[p], AF.Exp, scale=-0.5),
          waits=[("sp", CNT["sp"])], inc="sp")
        mark("lnS", "sp")
        E("v", lambda e: e.scalar_tensor_tensor(nmean[p], mv[p][:, 0:1], -1.0,
                                                rstd[p], ALU.mult, ALU.mult),
          waits=[lm("lnS"), ("vp", CNT["vp"])], inc="vp")
        mark("ln_chain", "vp")

    def emit_full_tables(dst_s, dst_c, mark_name, possrc, wextra=()):
        if possrc[0] == "ap":
            ib = possrc[1]
            E("v", lambda e, i=ib: e.tensor_scalar_mul(angb[:], theta_bc[:],
                                                       poscol_sb[:, i:i + 1]),
              waits=[lm("theta_bc"), lm("tiny_inputs"), ("gp", CNT["gp"]),
                     ("sp", CNT["sp"]), ("vp", CNT["vp"])] + list(wextra),
              inc="vp")
        else:
            f = possrc[1]
            E("v", lambda e, f=f: e.tensor_single_scalar(angb[:], theta_bc[:], f,
                                                         ALU.mult),
              waits=[lm("theta_bc"), ("gp", CNT["gp"]), ("sp", CNT["sp"]),
                     ("vp", CNT["vp"])] + list(wextra),
              inc="vp")
        E("v", lambda e: e.tensor_single_scalar(dst_c[:], angb[:], INV2PI, ALU.mult),
          waits=[("vp", CNT["vp"])], inc="vp")
        E("v", lambda e: e.tensor_copy(kint[:], dst_c[:]),
          waits=[("vp", CNT["vp"])], inc="vp")
        E("v", lambda e: e.tensor_copy(rm1[:], kint[:]),
          waits=[("vp", CNT["vp"])], inc="vp")
        E("v", lambda e: e.scalar_tensor_tensor(dst_c[:], rm1[:], -CW1, angb[:],
                                                ALU.mult, ALU.add),
          waits=[("vp", CNT["vp"])], inc="vp")
        E("v", lambda e: e.scalar_tensor_tensor(dst_c[:], rm1[:], -CW2, dst_c[:],
                                                ALU.mult, ALU.add),
          waits=[("vp", CNT["vp"])], inc="vp")
        E("v", lambda e: e.scalar_tensor_tensor(dst_s[:], rm1[:], -CW3, dst_c[:],
                                                ALU.mult, ALU.add),
          waits=[("vp", CNT["vp"])], inc="vp")
        E("v", lambda e: e.tensor_single_scalar(rm2[:], dst_s[:], PI, ALU.is_gt),
          waits=[("vp", CNT["vp"])], inc="vp")
        E("v", lambda e: e.scalar_tensor_tensor(dst_s[:], rm2[:], -CW1, dst_s[:],
                                                ALU.mult, ALU.add),
          waits=[("vp", CNT["vp"])], inc="vp")
        E("v", lambda e: e.scalar_tensor_tensor(dst_s[:], rm2[:], -CW23, dst_s[:],
                                                ALU.mult, ALU.add),
          waits=[("vp", CNT["vp"])], inc="vp")
        E("v", lambda e: e.tensor_scalar(dst_s[:], dst_s[:], -PI, PI, ALU.max, ALU.min),
          waits=[("vp", CNT["vp"])], inc="vp")
        mark("tbl_v", "vp")
        E("s", lambda e: e.activation(dst_c[:], dst_s[:], AF.Abs),
          waits=[lm("tbl_v")], inc="sp")
        E("s", lambda e: e.activation(dst_s[:], dst_s[:], AF.Sin), inc="sp")
        E("s", lambda e: e.activation(dst_c[:], dst_c[:], AF.Sin, bias=hpi, scale=-1.0),
          waits=[lm("consts")], inc="sp")
        mark(mark_name, "sp")
        RMF["w"] = ("vp", CNT["vp"])

    def emit_delta_tables(p, iblk, sp_edge=False, on_g=False):
        sbp, cbp = sinb[p], cosb[p]
        w = [lm(f"tblr_{iblk - 2}"), lm("d256"), RMF["w"]]
        if sp_edge:
            w.append(("sp", CNT["sp"]))
        if on_g:
            wg = w + [("vp", CNT["vp"])]
            E("g", lambda e: e.tensor_tensor(rm1[:], sbp[:], c256[:], ALU.mult),
              waits=wg, inc="gp")
            E("g", lambda e: e.tensor_tensor(rm2[:], cbp[:], s256[:], ALU.mult),
              inc="gp")
            E("g", lambda e: e.tensor_tensor(rm3[:], cbp[:], c256[:], ALU.mult),
              inc="gp")
            E("g", lambda e: e.tensor_tensor(rm4[:], sbp[:], s256[:], ALU.mult),
              inc="gp")
            E("g", lambda e: e.tensor_add(sbp[:], rm1[:], rm2[:]), inc="gp")
            E("g", lambda e: e.tensor_sub(cbp[:], rm3[:], rm4[:]), inc="gp")
            mark(f"tblr_{iblk}", "gp")
            RMF["w"] = ("gp", CNT["gp"])
            return
        E("g", lambda e: e.tensor_tensor(rm1[:], sbp[:], c256[:], ALU.mult),
          waits=w, inc="gp")
        E("g", lambda e: e.tensor_tensor(rm2[:], cbp[:], s256[:], ALU.mult),
          inc="gp")
        mark(f"dlt_{iblk}", "gp")
        E("v", lambda e: e.tensor_tensor(rm3[:], cbp[:], c256[:], ALU.mult),
          waits=w + [("gp", CNT["gp"])], inc="vp")
        E("v", lambda e: e.tensor_tensor(rm4[:], sbp[:], s256[:], ALU.mult),
          inc="vp")
        E("v", lambda e: e.tensor_add(sbp[:], rm1[:], rm2[:]),
          waits=[lm(f"dlt_{iblk}")], inc="vp")
        E("v", lambda e: e.tensor_sub(cbp[:], rm3[:], rm4[:]),
          inc="vp")
        mark(f"tblr_{iblk}", "vp")
        RMF["w"] = ("vp", CNT["vp"])

    def emit_rope(p, src, dst_bf, extra_waits, tag, tblk, sp_edge=False):
        sbp, cbp = sinb[p], cosb[p]
        w = [lm(f"tblr_{tblk}"), RMF["w"]] + list(extra_waits)
        if sp_edge:
            w.append(("sp", CNT["sp"]))
        E("g", lambda e: e.tensor_tensor(rm1[:], cbp[:], src[:, 0:512], ALU.mult),
          waits=w, inc="gp")
        E("g", lambda e: e.tensor_tensor(rm2[:], sbp[:], src[:, 512:1024], ALU.mult),
          waits=w, inc="gp")
        mark(f"rm12_{tag}", "gp")
        E("v", lambda e: e.tensor_tensor(rm3[:], cbp[:], src[:, 512:1024], ALU.mult),
          waits=w, inc="vp")
        E("v", lambda e: e.tensor_tensor(rm4[:], sbp[:], src[:, 0:512], ALU.mult),
          inc="vp")
        mark(f"rm34_{tag}", "vp")
        E("v", lambda e: e.tensor_sub(dst_bf[:, 0:512], rm1[:], rm2[:]),
          waits=[lm(f"rm12_{tag}")], inc="vp")
        E("v", lambda e: e.tensor_add(dst_bf[:, 512:1024], rm3[:], rm4[:]),
          inc="vp")
        RMF["w"] = ("vp", CNT["vp"])

    def emit_ksumkv(i):
        p = i & 1
        for n in range(2):
            E("t", lambda e, n=n, i=i, p=p: e.matmul(
                psF[0:1, 512 * n:512 * (n + 1)], ones1_bf[:],
                khb[p][:, 512 * n:512 * (n + 1)],
                start=(i == 0), stop=(i == NBS - 1), skip_group_check=True),
              waits=[lm(f"khb_{i}"), lm("consts")], inc="tp" if n == 1 else None)
        mark(f"khb_used_{i}", "tp")
        for h in range(H):
            E("t", lambda e, h=h, i=i, p=p, p3=i % 3: e.matmul(
                pkv[:, D * h:D * (h + 1)], kr_bf[p][:, D * h:D * (h + 1)],
                xn_bf[p3][:, D * h:D * (h + 1)],
                start=(i == 0 and h % 4 == 0), stop=(i == NBS - 1 and h == H - 1),
                skip_group_check=True),
              waits=[lm(f"kr_{i}"), lm(f"xn_{i}")] if h == 0 else (),
              inc="tp" if h == H - 1 else None)
        mark(f"kv_{i}", "tp")
        mark(f"kr_used_{i}", "tp")
        mark(f"xn_used_{i}", "tp")

    def emit_xload(i):
        w = [lm(f"xblk_free_{i - 2}")] if i >= 2 else []
        E("sy", lambda e, i=i, p=i & 1: e.dma_start(out=xblk[p][:], in_=xseq[128 * i:128 * (i + 1), :]),
          waits=w, inc="qx", inck=16, selfwait=False)
        mark(f"xload_{i}", "qx")

    def emit_kFx(i):
        p = i & 1
        p3 = i % 3
        emit_ln_stats(p, xblk[p], [lm(f"xload_{i}")])
        wxn = [lm("ln_chain"), lm(f"xload_{i}")]
        if i >= 3:
            wxn.append(lm(f"xn_used_{i - 3}"))
        if i == NBO + 3:
            wxn.append(lm("halo_r_dma"))
        E("s", lambda e: e.activation(xn_bf[p3][:], xblk[p][:], AF.Identity,
                                      scale=rstd[p], bias=nmean[p]),
          waits=wxn, inc="sp")
        mark(f"xn_{i}", "sp")
        mark(f"xblk_free_{i}", "sp")
        if i < NBO:
            dst = xnT[:, :, XNT_OFF + 128 * i: XNT_OFF + 128 * (i + 1)]
        else:
            dst = xnT_tmp[p][:, :, :]
        wtr = [lm(f"xn_{i}")]
        if i >= NBO + 2:
            wtr.append(lm(f"kstat_used_{i - 2}"))
        E("sy", lambda e, dst=dst, p3=p3: e.dma_start_transpose(dst, xn_bf[p3][:]),
          waits=wtr, inc="qt2", inck=16)
        mark(f"xnT_{i}", "qt2")
        if i + 2 < NBS:
            emit_xload(i + 2)

    def emit_kFp(i):
        p = i & 1
        wmm = [lm(f"xnT_{i}"), lm("wk_loaded"), lm("consts")]
        if i == FD + 1:
            wmm.append(lm("pk1_free_delta"))
        if i == 0:
            wmm.append(lm("pk0_free0"))
        if i >= 2:
            wmm.append(lm(f"pk_free_{i - 2}"))
        first = True
        for kc in range(9):
            if kc == 8:
                sap = ones_blk[:]
            elif i < NBO:
                sap = xnT[:, kc, XNT_OFF + 128 * i: XNT_OFF + 128 * (i + 1)]
            else:
                sap = xnT_tmp[p][:, kc, :]
            for n in range(2):
                isl = (kc == 8 and n == 1)
                E("t", lambda e, sap=sap, kc=kc, n=n, p=p: e.matmul(
                    pk[p][:, 512 * n:512 * (n + 1)], sap,
                    wk_bf[:, kc, 512 * n:512 * (n + 1)],
                    start=(kc == 0), stop=(kc == 8), skip_group_check=True),
                  waits=wmm if first else (), inc="tp" if isl else None)
                first = False
        mark(f"kproj_{i}", "tp")
        mark(f"kstat_used_{i}", "tp")
    def emit_kR1(i):
        p = i & 1
        wxm = [lm(f"kproj_{i}")]
        if i >= 2:
            wxm += [lm(f"khb_{i - 2}")]
        E("s", lambda e: e.activation(kh[p][:], pk[p][:], AF.Exp),
          waits=wxm, inc="sp")
        mark(f"kexp_{i}", "sp")
        E("v", lambda e: e.tensor_scalar_min(kh[p][:], kh[p][:], 1.0),
          waits=[lm(f"kexp_{i}")], inc="vp")
        wst = [lm(f"kproj_{i}")]
        if i >= 2:
            wst.append(lm(f"khb_used_{i - 2}"))
        E("v", lambda e: e.scalar_tensor_tensor(khb[p][:], pk[p][:], 0.0, kh[p][:],
                                                ALU.max, ALU.add),
          waits=wst, inc="vp")
        mark(f"kh_{i}", "vp")
        mark(f"khb_{i}", "vp")
        mark(f"pk_free_{i}", "vp")
    def emit_kR2(i):
        p = i & 1
        wrk = [lm(f"khb_{i}")]
        if i >= 2:
            wrk.append(lm(f"kr_used_{i - 2}"))
        emit_rope(p, khb[p], kr_bf[p], wrk, f"k{i}", i)
        mark(f"kr_{i}", "vp")
        if i + 2 < NBS:
            if i + 2 in (NBO, NBO + 1):
                emit_full_tables(sinb[p], cosb[p], f"tblr_{i + 2}", ("ap", i + 2))
            else:
                emit_delta_tables(p, i + 2)
        if i == NBO:
            E("sy", lambda e, p3=i % 3: e.dma_start(out=halo_scr[:], in_=xn_bf[p3][0:1, :]),
              waits=[lm(f"xn_{i}")], inc="qhl", inck=16)
            E("sy", lambda e: e.dma_start(
                out=xnT[:, :, COL_RH:COL_RH + 1],
                in_=halo_scr.rearrange("(b pp) -> pp b", pp=128)[:, :, None],
                allow_slow_non_contiguous=True),
              waits=["qhl"], inc="qhl", inck=16)
            mark("halo_r_dma", "qhl")
            E("v", lambda e: e.tensor_scalar_mul(
                xnT[:, :, COL_RH:COL_RH + 1], xnT[:, :, COL_RH:COL_RH + 1],
                masklr_sb[:, 1:2]),
              waits=[lm("halo_r_dma"), lm("tiny_inputs")], inc="vp")
            mark("halo_r", "vp")
        if i == NBS - 1:
            E("sy", lambda e, p3=i % 3: e.dma_start(out=halo_scr[:], in_=xn_bf[p3][127:128, :]),
              waits=[lm(f"xn_{i}")], inc="qhl", inck=16)
            E("sy", lambda e: e.dma_start(
                out=xnT[:, :, COL_LH:COL_LH + 1],
                in_=halo_scr.rearrange("(b pp) -> pp b", pp=128)[:, :, None],
                allow_slow_non_contiguous=True),
              waits=["qhl"], inc="qhl", inck=16)
            mark("halo_l_dma", "qhl")
            E("v", lambda e: e.tensor_scalar_mul(
                xnT[:, :, COL_LH:COL_LH + 1], xnT[:, :, COL_LH:COL_LH + 1],
                masklr_sb[:, 0:1]),
              waits=[lm("halo_l_dma"), lm("tiny_inputs")], inc="vp")
            mark("halo_l", "vp")
    sgn_c = x2b[:, 512:1024]
    FD = 10

    def emit_fdelta():
        E("t", lambda e: e.matmul(pk[1][0:1, 0:1], absacc, onesc_f32,
                                  start=True, stop=True, skip_group_check=True),
          waits=[lm("fstat_done"), lm("consts"), lm(f"pk_free_{FD - 1}")], inc="tp")
        mark("fdelta_ps", "tp")
        E("v", lambda e: e.tensor_scalar_mul(delta_s[:], pk[1][0:1, 0:1], 0.7 / float(C * C)),
          waits=[lm("fdelta_ps")], inc="vp")
        mark("fdelta_s", "vp")
        E("t", lambda e: e.matmul(pk[1][:, 0:1], ones_f32[:], delta_s[:],
                                  start=True, stop=True, skip_group_check=True),
          waits=[lm("fdelta_s")], inc="tp")
        mark("fdelta_bc", "tp")
        E("v", lambda e: e.tensor_copy(delta_bc, pk[1][:, 0:1]),
          waits=[lm("fdelta_bc")], inc="vp")
        mark("delta_bc", "vp")
        mark("pk1_free_delta", "vp")

    fjobs = list(range(18))
    fb_last = {"v": 0, "s": 0}

    def emit_fjob(j):
        kc, hf = j // 2, j % 2
        sl = np.s_[512 * hf:512 * (hf + 1)]
        wd = [("vp", fb_last["v"]), ("sp", fb_last["s"])]
        if j == 0:
            wd.append(("vp", stage_users[0]))
        E("s", lambda e, kc=kc, hf=hf: e.dma_start(
            out=stage[:], in_=wffn[kc, :, 512 * hf:512 * (hf + 1)]),
          waits=wd, inc="qc", inck=16, selfwait=False)
        mark("fb_dma", "qc")
        if kc < 8:
            wsg = [lm("fb_dma")]
            if j == 0:
                wsg.append(("vp", stage_users[1]))
            E("s", lambda e: e.sign(sgn_c[:], stage[:]), waits=wsg, inc="sp")
            E("s", lambda e: e.activation(stage[:], stage[:], AF.Abs), inc="sp")
            mark("fb_abs", "sp")
            E("v", lambda e: e.tensor_single_scalar(stage[:], stage[:], delta_bc,
                                                    ALU.is_gt),
              waits=[lm("fb_abs"), lm("delta_bc"), ("vp", CNT["vp"])], inc="vp")
            E("v", lambda e, kc=kc, sl=sl: e.scalar_tensor_tensor(
                wf_bf[:, kc, sl], sgn_c[:], alpha_sb, stage[:],
                ALU.mult, ALU.mult),
              waits=[lm("tiny_inputs")], inc="vp")
        else:
            E("v", lambda e, kc=kc, sl=sl: e.tensor_copy(wf_bf[:, kc, sl], stage[:]),
              waits=[lm("fb_dma"), ("vp", CNT["vp"])], inc="vp")
            if hf == 1:
                mark("ffn_built", "vp")
        fb_last["v"] = CNT["vp"]
        fb_last["s"] = CNT["sp"]

    emit_full_tables(s256, c256, "d256", ("imm", 256.0))
    emit_full_tables(sinb[0], cosb[0], "tblr_0", ("ap", 0))
    emit_full_tables(sinb[1], cosb[1], "tblr_1", ("ap", 1))
    emit_xload(0)
    emit_xload(1)
    emit_kFx(0)
    emit_kFx(1)
    emit_kFp(0)
    for i in range(NBS):
        if i >= 1:
            emit_ksumkv(i - 1)
        if i == FD:
            emit_fdelta()
        emit_kR1(i)
        if i + 2 < NBS:
            emit_kFx(i + 2)
        if i + 1 < NBS:
            emit_kFp(i + 1)
        emit_kR2(i)
        for _ in range(2):
            if wjobs:
                emit_wjob(wjobs.pop(0), 0)
        if i >= FD + 1 and fjobs:
            emit_fjob(fjobs.pop(0))
    emit_ksumkv(NBS - 1)

    while wjobs:
        emit_wjob(wjobs.pop(0), 0)

    E("s", lambda e: e.mul(kv_bf[:].rearrange("p h d -> p (h d)"), pkv[:], 1.0 / 64.0),
      waits=[lm(f"kv_{NBS - 1}")], inc="sp")
    mark("kv_bf", "sp")
    emit_full_tables(sinb[0], cosb[0], "tblr_0", ("ap", 0))
    emit_full_tables(sinb[1], cosb[1], "tblr_1", ("ap", 1))
    for n in range(2):
        E("v", lambda e, n=n: e.tensor_scalar_mul(srow[:], psF[0:1, 512 * n:512 * (n + 1)],
                                                  1.0 / float(NSEQ)),
          waits=[lm(f"khb_used_{NBS - 1}"), ("tp", CNT["tp"]), ("vp", CNT["vp"])], inc="vp")
        mark("kmean_row", "vp")
        E("t", lambda e, n=n: e.matmul(pk[0][:, 512 * n:512 * (n + 1)], ones_f32[:],
                                       srow[:], start=True, stop=True),
          waits=[lm("kmean_row"), lm(f"pk_free_{NBS - 2}")], inc="tp")
    mark("psF_free_boot", "vp")
    mark("kmb_ps", "tp")
    E("v", lambda e: e.tensor_copy(kmb[:], pk[0][:]), waits=[lm("kmb_ps")], inc="vp")
    mark("kmb", "vp")
    mark("pk_free_boot0", "vp")

    def emit_xres(i):
        w = [("qst", CNT["qst"])]
        if i < 2:
            w.append(lm(f"xblk_free_{NBS - 2 + i}"))
        E("sy", lambda e, i=i, p=i & 1: e.dma_start(out=xblk[p][:], in_=xseq[128 * i:128 * (i + 1), :]),
          waits=w, inc="qx", inck=16, selfwait=False)
        mark(f"xres_{i}", "qx")

    def emit_qA(i):
        p = i & 1
        wmm = [lm("wq_loaded")]
        if i == 0:
            wmm.append(lm("pk_free_boot0"))
        else:
            wmm.append(lm(f"qpk_free_{i - 1}"))
        first = True
        for kc in range(9):
            sap = (lambda kc=kc, i=i: ones_blk[:] if kc == 8 else
                   xnT[:, kc, XNT_OFF + 128 * i: XNT_OFF + 128 * (i + 1)])()
            for n in range(2):
                isl = (kc == 8 and n == 1)
                E("t", lambda e, sap=sap, kc=kc, n=n: e.matmul(
                    pk[0][:, 512 * n:512 * (n + 1)], sap,
                    wq_bf[:, kc, 512 * n:512 * (n + 1)],
                    start=(kc == 0), stop=(kc == 8)),
                  waits=wmm if first else (), inc="tp" if isl else None)
                first = False
        mark(f"qproj_{i}", "tp")
        wxm = [lm(f"qproj_{i}")]
        if i >= 2:
            wxm += [lm(f"qh_{i - 2}"), lm(f"rm34_q{i - 2}")]
        else:
            wxm += [lm(f"khb_{NBS - 2 + (i & 1)}")]
        E("s", lambda e: e.activation(kh[p][:], pk[0][:], AF.Exp),
          waits=wxm, inc="sp")
        mark(f"qexp_{i}", "sp")
        E("v", lambda e: e.tensor_scalar_min(kh[p][:], kh[p][:], 1.0),
          waits=[lm(f"qexp_{i}")], inc="vp")
        E("v", lambda e: e.scalar_tensor_tensor(kh[p][:], pk[0][:], 0.0, kh[p][:],
                                                ALU.max, ALU.add),
          waits=[lm(f"qproj_{i}")], inc="vp")
        mark(f"qh_{i}", "vp")
        mark(f"qpk_free_{i}", "vp")
        E("v", lambda e: e.tensor_tensor(qzb[:], kh[p][:], kmb[:], ALU.mult),
          waits=[lm("kmb"), lm(f"qh_{i}"), ("qst", CNT["qst"])], inc="vp")
        mark(f"prod_{i}", "vp")
        E("v", lambda e: e.tensor_reduce(
            zden[p], qzb.rearrange("pp (s h j) -> pp h s j", s=2, h=H, j=64),
            AX.XY, ALU.add), waits=[lm(f"prod_{i}"), ("vp", CNT["vp"])], inc="vp")
        E("v", lambda e: e.tensor_scalar_add(zden[p], zden[p], 1e-6),
          waits=[("vp", CNT["vp"])], inc="vp")
        E("v", lambda e: e.reciprocal(zval[p], zden[p]),
          waits=[("vp", CNT["vp"])], inc="vp")
        mark(f"z_{i}", "vp")
        if DEBUG:
            E("s", lambda e, i=i: e.dma_start(out=dbg_sm[i], in_=smallf[:, 0:80]),
              waits=[lm(f"z_{i}")], inc="qst", inck=16, selfwait=False)
        wr = [lm(f"qh_{i}")]
        if i == 0:
            wr.append(lm(f"khb_used_{NBS - 1}"))
        emit_rope(p, kh[p], qzb, wr, f"q{i}", i, sp_edge=(i < 2))
        mark(f"qr_{i}", "vp")
        wqz = [lm(f"qr_{i}"), lm(f"z_{i}")]
        if i >= 2:
            wqz.append(lm(f"qrz_used_{i - 2}"))
        else:
            wqz.append(lm(f"khb_used_{NBS - 2 + (i & 1)}"))
        E("v", lambda e, p=p: e.tensor_tensor(
            khb[p].rearrange("pp (h j) -> pp h j", h=H),
            qzb.rearrange("pp (h j) -> pp h j", h=H),
            zval[p][:, :, None].broadcast_to((128, H, D)), ALU.mult),
          waits=wqz, inc="vp")
        mark(f"qrz_{i}", "vp")
        if DEBUG:
            E("s", lambda e, i=i: e.dma_start(out=dbg_qz[128 * i:128 * (i + 1), :],
                                              in_=qzb[:]),
              waits=[lm(f"qr_{i}")], inc="qst", inck=16, selfwait=False)
        wtr = [lm(f"qrz_{i}")]
        if i >= 2:
            wtr.append(lm(f"qrT_used_{i - 2}"))
        else:
            wtr.append(lm(f"kstat_used_{NBS - 2 + (i & 1)}"))
        E("sy", lambda e, p=p: e.dma_start_transpose(xnT_tmp[p][:, :, :], khb[p][:]),
          waits=wtr, inc="qt3", inck=16)
        mark(f"qrT_{i}", "qt3")
        mark(f"qrz_used_{i}", "qt3")
        if i + 2 < NBO:
            emit_delta_tables(p, i + 2, sp_edge=(i < 2), on_g=True)
    def emit_qL(i):
        p = i & 1
        LL = [pkv, psF][p]
        wl = [lm("lepe_loaded"), lm("halo_l"), lm("halo_r")]
        if i == 0:
            wl.append(lm("kv_bf"))
        elif i == 1:
            wl.append(lm("psF_free_boot"))
        if i >= 2:
            wl.append(lm(f"ata2_{i - 2}"))
        first = True
        for t in range(3):
            for kc in range(9):
                if kc == 8 and t != 1:
                    continue
                sap = (lambda t=t, kc=kc, i=i: ones_blk[:] if kc == 8 else
                       xnT[:, kc, XNT_OFF + 128 * i + t - 1: XNT_OFF + 128 * i + t - 1 + 128])()
                for n in range(2):
                    isl = (t == 2 and kc == 7 and n == 1)
                    E("t", lambda e, sap=sap, li=lidx(t, kc), t=t, kc=kc, n=n, LL=LL: e.matmul(
                        LL[:, 512 * n:512 * (n + 1)], sap,
                        wl_bf[:, li, 512 * n:512 * (n + 1)],
                        start=(t == 0 and kc == 0), stop=False,
                        skip_group_check=True),
                      waits=wl if first else (), inc="tp" if isl else None)
                    first = False
        mark(f"lepe_{i}", "tp")
    def emit_qAT(i):
        p = i & 1
        LL = [pkv, psF][p]
        for h in range(H):
            E("t", lambda e, h=h, p=p, LL=LL: e.matmul(
                LL[:, D * h:D * (h + 1)], xnT_tmp[p][:, h, :], kv_bf[:, h, :],
                start=False, stop=(h == H - 1), skip_group_check=True),
              waits=[lm(f"qrT_{i}"), lm("kv_bf"), lm(f"lepe_{i}")] if h == 0 else (),
              inc="tp" if h == H - 1 else None)
        mark(f"attn_{i}", "tp")
        mark(f"qrT_used_{i}", "tp")
    B1LN2 = {}
    B2ATA = {}

    def emit_qB1h(i):
        p = i & 1
        LL = [pkv, psF][p]
        if DEBUG:
            E("v", lambda e, LL=LL: e.tensor_copy(x2b[:], LL[:]),
              waits=[lm(f"attn_{i}"), ("qst", CNT["qst"]), ("vp", CNT["vp"])], inc="vp")
            mark(f"dbgat_{i}", "vp")
            E("s", lambda e, i=i: e.dma_start(out=dbg_at[128 * i:128 * (i + 1), :],
                                              in_=x2b[:]),
              waits=[lm(f"dbgat_{i}")], inc="qst", inck=16, selfwait=False)
        emit_ln_stats(p, LL, [lm(f"attn_{i}")])
        B1LN2[i] = lm("ln_chain")

    def emit_qB1(i):
        p = i & 1
        LL = [pkv, psF][p]
        wla = [lm(f"attn_{i}"), B1LN2[i], ("sp", CNT["sp"]), ("gp", CNT["gp"])]
        if i >= 1:
            wla.append(B2ATA[i - 1])
        E("s", lambda e, LL=LL: e.activation(x2b[:], LL[:], AF.Identity,
                                             scale=rstd[p], bias=nmean[p]),
          waits=wla, inc="sp")
        mark(f"ln2aff_{i}", "sp")
        wat = [lm(f"xres_{i}"), lm(f"ln2aff_{i}")]
        if DEBUG:
            wat.append(("qst", CNT["qst"]))
        if i >= 1:
            wat.append(lm(f"y3_{i - 1}"))
        E("v", lambda e, p=p: e.tensor_add(x2b[:], x2b[:], xblk[p][:]),
          waits=wat, inc="vp")
        mark(f"ata2_{i}", "vp")
        if DEBUG:
            E("s", lambda e, i=i: e.dma_start(out=dbg_x2[128 * i:128 * (i + 1), :],
                                              in_=x2b[:]),
              waits=[lm(f"ata2_{i}")], inc="qst", inck=16, selfwait=False)
        emit_ln_stats(p, x2b, [lm(f"ata2_{i}")])
        wy = [lm("ln_chain"), lm(f"ata2_{i}")]
        if i >= 2:
            wy.append(lm(f"y3_used_{i - 2}"))
        else:
            wy.append(lm(f"xn_used_{NBS - 2 + (i & 1)}"))
            wy.append(lm("halo_r_dma") if (i & 1) == 0 else lm("halo_l_dma"))
        E("s", lambda e, p=p: e.activation(xn_bf[p][:], x2b[:], AF.Identity,
                                           scale=rstd[p], bias=nmean[p]),
          waits=wy, inc="sp")
        mark(f"y3_{i}", "sp")
        wtr = [lm(f"y3_{i}")]
        if i >= 2:
            wtr.append(lm(f"y3T_used_{i - 2}"))
        else:
            wtr.append(lm(f"kr_used_{NBS - 2 + (i & 1)}"))
        E("sy", lambda e, p=p: e.dma_start_transpose(
            kr_bf[p].rearrange("pp (b j) -> pp b j", b=8), xn_bf[p][:]),
          waits=wtr, inc="qty", inck=16)
        mark(f"y3T_{i}", "qty")
        mark(f"y3_used_{i}", "qty")
    def emit_qB2(i):
        p = i & 1
        wf = [lm("ffn_built"), lm(f"y3T_{i}")]
        if i == 0:
            wf.append(lm(f"pk_free_{NBS - 1}"))
        else:
            wf.append(lm(f"ata4_{i - 1}"))
        first = True
        for kc in range(9):
            sap = (lambda kc=kc, p=p: ones_blk[:] if kc == 8 else
                   kr_bf[p].rearrange("pp (b j) -> pp b j", b=8)[:, kc, :])()
            for n in range(2):
                isl = (kc == 8 and n == 1)
                E("t", lambda e, sap=sap, kc=kc, n=n: e.matmul(
                    pk[1][:, 512 * n:512 * (n + 1)], sap,
                    wf_bf[:, kc, 512 * n:512 * (n + 1)],
                    start=(kc == 0), stop=(kc == 8), skip_group_check=True),
                  waits=wf if first else (), inc="tp" if isl else None)
                first = False
        mark(f"ffn_{i}", "tp")
        mark(f"y3T_used_{i}", "tp")
        emit_ln_stats(p, pk[1], [lm(f"ffn_{i}")])
        E("s", lambda e, p=p: e.activation(xblk[p][:], pk[1][:], AF.Identity,
                                           scale=rstd[p], bias=nmean[p]),
          waits=[lm(f"ffn_{i}"), lm("ln_chain"), ("sp", CNT["sp"]),
                 ("gp", CNT["gp"])], inc="sp")
        mark(f"ln4aff_{i}", "sp")
        E("v", lambda e, p=p: e.tensor_add(xblk[p][:], xblk[p][:], x2b[:]),
          waits=[lm(f"ln4aff_{i}")], inc="vp")
        mark(f"ata4_{i}", "vp")
        B2ATA[i] = ("vp", CNT["vp"])
        E("s", lambda e, i=i, p=p: e.dma_start(out=out[128 * i:128 * (i + 1), :],
                                               in_=xblk[p][:]),
          waits=[lm(f"ata4_{i}")], inc="qst", inck=16, selfwait=False)
        mark(f"outsb_used_{i}", "qst")

    emit_qA(0)
    emit_qL(0)
    emit_qAT(0)
    emit_xres(0)
    for i in range(1, NBO):
        emit_qB1h(i - 1)
        emit_qA(i)
        if i >= 2:
            emit_qB2(i - 2)
        emit_qB1(i - 1)
        emit_qL(i)
        emit_qAT(i)
        emit_xres(i)
    emit_qB1h(NBO - 1)
    emit_qB2(NBO - 2)
    emit_qB1(NBO - 1)
    emit_qB2(NBO - 1)


    final_qst = CNT["qst"]
    OPS["s"].append(lambda e: e.wait_ge(SEM["qst"], final_qst))

    with nc.Block() as block:
        @block.vector
        def _(v):
            for f in OPS["v"]:
                f(v)

        @block.scalar
        def _(s):
            for f in OPS["s"]:
                f(s)

        @block.tensor
        def _(t):
            for f in OPS["t"]:
                f(t)

        @block.gpsimd
        def _(g):
            for f in OPS["g"]:
                f(g)

        @block.sync
        def _(sy):
            for f in OPS["sy"]:
                f(sy)

    ctx.close()
    return nc


def _host_prep(L0_attn_x, L0_attn_qk_w, L0_attn_qk_b, L0_attn_lepe_w, L0_attn_lepe_b,
               L0_ffn_weight, L0_ffn_alpha, L0_ffn_bias):
    x = np.ascontiguousarray(np.asarray(L0_attn_x, dtype=np.float32))
    qk_w = np.asarray(L0_attn_qk_w, dtype=np.float32)
    qk_b = np.asarray(L0_attn_qk_b, dtype=np.float32)
    lepe_w = np.asarray(L0_attn_lepe_w, dtype=np.float32)
    lepe_b = np.asarray(L0_attn_lepe_b, dtype=np.float32)
    ffn_w = np.asarray(L0_ffn_weight, dtype=np.float32)
    ffn_a = np.asarray(L0_ffn_alpha, dtype=np.float32)
    ffn_b = np.asarray(L0_ffn_bias, dtype=np.float32)

    perm = np.concatenate([np.arange(0, C, 2), np.arange(1, C, 2)])
    qk_w_p = np.concatenate([qk_w[:, :C][:, perm], qk_w[:, C:][:, perm]], axis=1)
    qk_b_p = np.concatenate([qk_b[:C][perm], qk_b[C:][perm]])

    wqk9 = np.zeros((9, 128, 2 * C), np.float32)
    wqk9[:8] = qk_w_p.reshape(8, 128, 2 * C)
    wqk9[8, 0, :] = qk_b_p

    wlepe9 = np.zeros((3, 9, 128, C), np.float32)
    wlepe9[:, :8] = lepe_w.reshape(3, 8, 128, C)
    wlepe9[1, 8, 0, :] = lepe_b

    wffn9 = np.zeros((9, 128, C), np.float32)
    wffn9[:8] = np.ascontiguousarray(ffn_w.T).reshape(8, 128, C)
    wffn9[8, 0, :] = ffn_b

    alpha_bc = np.ascontiguousarray(np.broadcast_to(ffn_a.reshape(1, 1), (128, 1)),
                                    dtype=np.float32)

    in_maps = []
    for c in range(8):
        b, half = c // 2, c % 2
        pos0 = half * TOWN
        xrot = np.ascontiguousarray(np.concatenate([x[b, pos0:], x[b, :pos0]], axis=0))
        pc = np.empty((128, NBS), np.float32)
        idx = (pos0 + np.arange(NSEQ)) % NSEQ
        pc[:, :] = idx.reshape(NBS, 128).T
        mk = np.zeros((128, 2), np.float32)
        mk[:, 0] = 1.0 if pos0 > 0 else 0.0
        mk[:, 1] = 1.0 if pos0 == 0 else 0.0
        in_maps.append({
            "xseq": xrot, "wqk": wqk9, "wlepe": wlepe9, "wffn": wffn9,
            "alpha": alpha_bc, "poscol": np.ascontiguousarray(pc),
            "masklr": mk,
        })
    return in_maps


def kernel(L0_attn_x, L0_attn_qk_w, L0_attn_qk_b, L0_attn_lepe_w, L0_attn_lepe_b,
           L0_ffn_weight, L0_ffn_alpha, L0_ffn_bias):
    in_maps = _host_prep(L0_attn_x, L0_attn_qk_w, L0_attn_qk_b, L0_attn_lepe_w,
                         L0_attn_lepe_b, L0_ffn_weight, L0_ffn_alpha, L0_ffn_bias)
    nc = build_nc()
    res = run_bass_kernel_spmd(nc, in_maps, core_ids=list(range(8)))
    out = np.empty((4, NSEQ, C), np.float32)
    for c in range(8):
        b, half = c // 2, c % 2
        out[b, half * TOWN:(half + 1) * TOWN] = res.results[c]["out"]
    return out
